# version 13
# baseline (speedup 1.0000x reference)
import numpy as np
import ml_dtypes

import concourse.bass as bass
import concourse.mybir as mybir
from concourse.bass_utils import run_bass_kernel_spmd

F32 = mybir.dt.float32
BF16 = mybir.dt.bfloat16
F16 = mybir.dt.float16
I32 = mybir.dt.int32
AF = mybir.ActivationFunctionType
ALU = mybir.AluOpType
BF = ml_dtypes.bfloat16

S = 4096
D = 2048
NH, KVH, DH = 16, 4, 128
II = 5632
WIN = 128
SPW = 512
NSLOT = 4
SLOT_COUNTS = [9, 17, 25, 32]
SLOT_F = [1, 9, 17, 25]
SPANS_A = [0, 2, 5, 7]
SPANS_B = [1, 3, 4, 6]
SCALE = float(DH ** -0.5)
EXP_BIAS = -4.0
P126 = float(2.0 ** 126)
INV_D = 1.0 / D

pi_d = np.array([4 * (p % 512) + p // 512 for p in range(D)])
pi_j = np.array([4 * (p % 1408) + p // 1408 for p in range(II)])


def build_program():
    nc = bass.Bass("TRN2", target_bir_lowering=False)

    x_in = nc.dram_tensor("x_in", [8, 128, 16, SPW], F32, kind="ExternalInput").ap()
    xo_in = nc.dram_tensor("xo_in", [NSLOT, 128, 16, SPW], F32, kind="ExternalInput").ap()
    wq_in = nc.dram_tensor("wq_in", [NH, 128, 16, 128], F32, kind="ExternalInput").ap()
    wo_in = nc.dram_tensor("wo_in", [16, 128, NH, 128], F32, kind="ExternalInput").ap()
    wk_in = nc.dram_tensor("wk_in", [128, 16, 512], F32, kind="ExternalInput").ap()
    wv_in = nc.dram_tensor("wv_in", [128, 16, 512], F32, kind="ExternalInput").ap()
    gup_in = nc.dram_tensor("gup_in", [88, 128, 256], I32, kind="ExternalInput").ap()
    gus_in = nc.dram_tensor("gus_in", [88, 128, 16], F16, kind="ExternalInput").ap()
    dnp_in = nc.dram_tensor("dnp_in", [16, 128, 704], I32, kind="ExternalInput").ap()
    dns_in = nc.dram_tensor("dns_in", [16, 128, 44], F16, kind="ExternalInput").ap()
    cosk_in = nc.dram_tensor("cosk_in", [128, S], F32, kind="ExternalInput").ap()
    sink_in = nc.dram_tensor("sink_in", [128, S], F32, kind="ExternalInput").ap()
    cosq_in = nc.dram_tensor("cosq_in", [NSLOT, 128, SPW], F32, kind="ExternalInput").ap()
    sinq_in = nc.dram_tensor("sinq_in", [NSLOT, 128, SPW], F32, kind="ExternalInput").ap()
    pswap_in = nc.dram_tensor("pswap_in", [128, 128], BF16, kind="ExternalInput").ap()
    masks_in = nc.dram_tensor("masks_in", [NSLOT, 128, 8, SPW], BF16, kind="ExternalInput").ap()
    w3g = nc.dram_tensor("w3g", [22, 128, 16, 512], BF16, kind="Internal").ap()
    w3d = nc.dram_tensor("w3d", [44, 128, 2048], BF16, kind="Internal").ap()
    ident_in = nc.dram_tensor("ident_in", [128, 128], BF16, kind="ExternalInput").ap()
    out_t = nc.dram_tensor("out_t", [128, 16, 4 * SPW], F32, kind="ExternalOutput").ap()

    class Sem:
        def __init__(self, name):
            self.h = nc.alloc_semaphore(name)
            self.v = 0

    sm = {n: Sem(n) for n in
          ["dPK0", "dPK1", "dX0", "dX1", "dXO", "dXF", "dW", "dWq0", "dWq1",
           "dM", "dKT", "dSP0", "dSP1", "dRL0", "dRL1", "dRL20", "dRL21",
           "dGS0", "dGS1",
           "dOUT",
           "vD", "vK", "vQ", "vA", "vE", "vO", "vF", "vY", "aE", "aS", "aD",
           "tS", "tAV", "tKV", "tQ", "tWo", "tF", "vCH", "aCH"]}

    _CHAINED = {"tensor_tensor", "tensor_scalar", "tensor_single_scalar",
                "scalar_tensor_tensor", "tensor_copy", "tensor_add",
                "tensor_sub", "tensor_mul", "tensor_max", "tensor_scalar_mul",
                "tensor_scalar_add", "memset", "reciprocal", "activation",
                "mul", "add", "copy", "square", "sqrt"}

    import collections

    class Chain:

        _hist = {}

        def __init__(self, eng, sem_name):
            self._e = eng
            self._s = sm[sem_name]
            self._name = sem_name
            self._pending = None
            self._h = Chain._hist.setdefault(sem_name,
                                             collections.deque(maxlen=16))

        def _flush(self):
            if self._pending is not None:
                self._pending.then_inc(self._s.h, 1)
                self._s.v += 1
                self._h.append((self._s.h, self._s.v))
                self._pending = None

        def mark(self, sem_name, inc=1):
            assert self._pending is not None
            ms = sm[sem_name]
            self._pending.then_inc(ms.h, inc)
            ms.v += inc
            self._h.append((ms.h, ms.v))
            self._pending = None

        def finish(self):
            self._flush()
            if self._h:
                self._e.wait_ge(*self._h[-1])

        def __getattr__(self, name):
            f = getattr(self._e, name)
            if name not in _CHAINED:
                return f

            def wrap(*a, _lag=1, **kw):
                self._flush()
                if _lag is not None and len(self._h) >= _lag:
                    self._e.wait_ge(*self._h[-_lag])
                ins = f(*a, **kw)
                self._pending = ins
                return ins
            return wrap

    K_t = nc.alloc_sbuf_tensor("K_t", [128, KVH, S], BF16)
    V_t = nc.alloc_sbuf_tensor("V_t", [128, 32, 512], BF16)
    pswap = nc.alloc_sbuf_tensor("pswap", [128, 128], BF16)
    ones_b = nc.alloc_sbuf_tensor("ones_b", [128, 1], BF16)
    ones_f = nc.alloc_sbuf_tensor("ones_f", [128, 1], F32)
    ones_k1 = nc.alloc_sbuf_tensor("ones_k1", [1, 128], F32)
    s6 = nc.alloc_sbuf_tensor("s6", [128, 1], I32)
    cb4 = nc.alloc_sbuf_tensor("cb4", [128, 1], F32)
    ceps1 = nc.alloc_sbuf_tensor("ceps1", [128, 1], F32)
    ceps2 = nc.alloc_sbuf_tensor("ceps2", [128, 1], F32)
    xob16 = nc.alloc_sbuf_tensor("xob16", [128, 16, SPW], BF16)
    ident = nc.alloc_sbuf_tensor("ident", [128, 128], BF16)

    DTSZ = {F32: 4, BF16: 2, F16: 2, I32: 4}
    SBUF_TOP = 229344
    overlay_specs = [
        ("pk0", [128, 704], I32, {"W"}), ("pk1", [128, 704], I32, {"W"}),
        ("sc0", [128, 44], F16, {"W"}), ("sc1", [128, 44], F16, {"W"}),
        ("scf0", [128, 44], F32, {"W"}), ("scf1", [128, 44], F32, {"W"}),
        ("u7t", [128, 4, 704], I32, {"W"}), ("sgt", [128, 4, 704], I32, {"W"}),
        ("bitst", [128, 4, 704], I32, {"W"}),
        ("dec0", [128, 4, 1408], BF16, {"W"}), ("dec1", [128, 4, 1408], BF16, {"W"}),
        ("tsp0", [128, 44, 128], BF16, {"W"}), ("tsp1", [128, 44, 128], BF16, {"W"}),
        ("xb0", [128, 16, 512], BF16, {"P0"}), ("xb1", [128, 16, 512], BF16, {"P0"}),
        ("wk_t", [128, 16, 512], BF16, {"P0"}), ("wv_t", [128, 16, 512], BF16, {"P0"}),
        ("cosk_t", [128, S], F32, {"P0"}), ("sink_t", [128, S], F32, {"P0"}),
        ("masks_t", [128, 8, SPW], BF16, {"Q", "A"}),
        ("cosq_t", [128, SPW], F32, {"Q"}), ("sinq_t", [128, SPW], F32, {"Q"}),
        ("wblk0", [128, 16, 128], BF16, {"Q", "O"}),
        ("wblk1", [128, 16, 128], BF16, {"Q", "O"}),
        ("Q_t", [128, NH, SPW], BF16, {"Q", "A"}),

        ("On_t", [128, NH, SPW], BF16, {"A", "O"}),
        ("xof32", [128, 16, SPW], F32, {"O", "L2"}),
        ("x2b", [128, 16, SPW], BF16, {"O", "Fg", "Fd", "L2"}),
        ("h_t", [128, 44, SPW], BF16, {"Fg", "Fd"}),
        ("gslab0", [128, 16, 512], BF16, {"O", "Fg"}),
        ("gslab1", [128, 16, 512], BF16, {"Fg"}),
        ("dslab0", [128, 1024], BF16, {"Fd"}), ("dslab1", [128, 1024], BF16, {"Fd"}),
        ("yf", [128, 16, SPW], F32, {"Fd", "L2"}),
        ("lrow", [1, SPW], F32, {"A"}), ("lrec", [1, SPW], F32, {"A"}),
        ("lln", [1, SPW], F32, {"A"}),
        ("r0", [1, SPW], F32, {"O", "L2"}), ("r1", [1, SPW], F32, {"O", "L2"}),
        ("r2", [1, SPW], F32, {"O", "L2"}),
        ("bc0", [128, SPW], F32, {"A", "O", "L2"}),
        ("bc1", [128, SPW], F32, {"O", "L2"}),
        ("P_t0", [128, 2 * SPW], BF16, {"A"}), ("P_t1", [128, 2 * SPW], BF16, {"A"}),
        ("qtmp", [128, SPW], BF16, {"P0", "Q"}),
        ("t1f", [128, SPW], F32, {"P0", "Q"}), ("t2f", [128, SPW], F32, {"P0", "Q"}),
        ("sqs0", [128, SPW], F32, {"O", "L2"}), ("sqs1", [128, SPW], F32, {"O", "L2"}),
        ("silu0", [128, SPW], F32, {"Fg"}), ("silu1", [128, SPW], F32, {"Fg"}),
    ]
    used = []
    placed = {}
    specs = sorted(overlay_specs, key=lambda t: -int(np.prod(t[1][1:])) * DTSZ[t[2]])
    for name, shape, dt_, phases in specs:
        nbytes = int(np.prod(shape[1:])) * DTSZ[dt_]
        nbytes = (nbytes + 31) // 32 * 32
        off = None
        cand = 0
        while True:
            conflict = None
            for (s0_, e0_, ph0) in used:
                if cand < e0_ and cand + nbytes > s0_ and (phases & ph0):
                    conflict = e0_
                    break
            if conflict is None:
                off = cand
                break
            cand = conflict
        used.append((off, off + nbytes, set(phases)))
        placed[name] = (off, shape, dt_)
    ov_size = max(e for _, e, _ in used)
    ov_base = SBUF_TOP - ov_size
    rem = nc.sbuf_bytes_remaining
    assert ov_size + 8192 <= rem, (ov_size, rem)
    ov = {}
    for name, (off, shape, dt_) in placed.items():
        ov[name] = nc.alloc_sbuf_tensor_at(name, shape, dt_, offset=ov_base + off)

    masks_t, cosq_t, sinq_t = ov["masks_t"], ov["cosq_t"], ov["sinq_t"]
    Q_t, On_t = ov["Q_t"], ov["On_t"]
    xof32, x2b, yf, h_t = ov["xof32"], ov["x2b"], ov["yf"], ov["h_t"]
    wblk = [ov["wblk0"], ov["wblk1"]]
    gslab = [ov["gslab0"], ov["gslab1"]]
    dslab = [ov["dslab0"], ov["dslab1"]]
    lrow, lrec, lln = ov["lrow"], ov["lrec"], ov["lln"]
    r0, r1, r2 = ov["r0"], ov["r1"], ov["r2"]
    bc0, bc1 = ov["bc0"], ov["bc1"]
    P_t = [ov["P_t0"], ov["P_t1"]]
    qtmp, t1f, t2f = ov["qtmp"], ov["t1f"], ov["t2f"]
    sqs = [ov["sqs0"], ov["sqs1"]]
    silu_sb = [ov["silu0"], ov["silu1"]]

    with nc.Block() as blk:
        @blk.vector
        def _(v):
            v = Chain(v, "vCH")
            v.memset(s6[:], 6)
            v.memset(cb4[:], EXP_BIAS)
            v.memset(ceps1[:], 1e-5)
            v.memset(ceps2[:], 1e-6)
            v.memset(ones_b[:], 1.0)
            v.memset(ones_f[:], 1.0)
            v.memset(ones_k1[:], 1.0)
            v.finish()

        @blk.sync
        def _(sy):
            sy.dma_start(out=pswap[:], in_=pswap_in[:, :]).then_inc(sm["dM"].h, 16)
            sy.dma_start(out=ident[:], in_=ident_in[:, :]).then_inc(sm["dM"].h, 16)

        @blk.gpsimd
        def _(gp):
            gp.dma_start(out=xob16[:], in_=xo_in[0]).then_inc(sm["dXO"].h, 16)
    sm["dM"].v += 32
    sm["dXO"].v += 16
    nc.all_engine_barrier()

    chunks = [("g", c, 256, 16) for c in range(88)] + [("d", c, 704, 44) for c in range(16)]
    NCH = len(chunks)
    grp_of_chunk = []
    gtot = 0
    for (kind, c, W, G) in chunks:
        nj = (2 * W * 4) // 128 // 4
        grp_of_chunk.append((gtot, nj))
        gtot += nj
    bPK = [sm["dPK0"].v, sm["dPK1"].v]
    bSP = [sm["dSP0"].v, sm["dSP1"].v]
    bVD, bTT, bVT = sm["vD"].v, sm["tKV"].v, sm["vK"].v
    bAD = sm["aD"].v
    dPK = [sm["dPK0"].h, sm["dPK1"].h]
    dSP = [sm["dSP0"].h, sm["dSP1"].h]
    scfb = [ov["scf0"], ov["scf1"]]
    u7t, sgt, bitst = ov["u7t"], ov["sgt"], ov["bitst"]
    with (
        nc.psum_tensor([128, 1024], BF16) as tp0,
        nc.psum_tensor([128, 1024], BF16) as tp1,
        nc.Block() as blk,
    ):
        tps_ = [tp0, tp1]
        pk = [ov["pk0"], ov["pk1"]]
        scb = [ov["sc0"], ov["sc1"]]
        dec = [ov["dec0"], ov["dec1"]]
        tsp = [ov["tsp0"], ov["tsp1"]]

        @blk.sync
        def _(sy):
            for ci, (kind, c, W, G) in enumerate(chunks):
                if ci >= 2:
                    sy.wait_ge(sm["vD"].h, bVD + ci - 1)
                    sy.wait_ge(sm["aD"].h, bAD + ci - 1)
                src_p = gup_in[c] if kind == "g" else dnp_in[c]
                src_s = gus_in[c] if kind == "g" else dns_in[c]
                sy.dma_start(out=pk[ci % 2][:, :W], in_=src_p).then_inc(dPK[ci % 2], 16)
                sy.dma_start(out=scb[ci % 2][:, :G], in_=src_s).then_inc(dPK[ci % 2], 16)

        @blk.vector
        def _(v):
            v = Chain(v, "vCH")
            for ci, (kind, c, W, G) in enumerate(chunks):
                g0, ng = grp_of_chunk[ci]
                v.wait_ge(dPK[ci % 2], bPK[ci % 2] + 32 * (ci // 2 + 1))
                if ci >= 2:
                    v.wait_ge(sm["tKV"].h, bTT + grp_of_chunk[ci - 2][0]
                              + grp_of_chunk[ci - 2][1])
                x = pk[ci % 2][:, :W]
                scf = scfb[ci % 2]
                for k in range(4):
                    v.tensor_scalar(u7t[:, k, :W], x, 4 * k, 0x00070007,
                                    ALU.logical_shift_right, ALU.bitwise_and,
                                    _lag=4)
                for k in range(4):
                    v.tensor_scalar(sgt[:, k, :W], x, 12 - 4 * k, -2147450880,
                                    ALU.logical_shift_left, ALU.bitwise_and,
                                    _lag=4)
                for k in range(4):
                    v.scalar_tensor_tensor(bitst[:, k, :W], u7t[:, k, :W],
                                           s6[:, 0:1], sgt[:, k, :W],
                                           ALU.logical_shift_left,
                                           ALU.bitwise_or, _lag=4)
                v.wait_ge(sm["aD"].h, bAD + ci + 1)
                for k in range(4):
                    ins = v.tensor_tensor(
                        dec[ci % 2][:, k, :2 * W].rearrange("p (g j) -> p g j", j=32),
                        bitst[:, k, :W].bitcast(BF16).rearrange("p (g j) -> p g j", j=32),
                        bass.AP(scf.ap().tensor, scf.ap().offset,
                                [scf.ap().ap[0], [1, G], [0, 32]]),
                        ALU.mult, _lag=4)
                v.mark("vD")
            v.finish()

        @blk.scalar
        def _(sc):
            sc = Chain(sc, "aCH")

            def emit_scf(cj):
                _, _, _, Gj = chunks[cj]
                sc.wait_ge(dPK[cj % 2], bPK[cj % 2] + 32 * (cj // 2 + 1))
                if cj >= 2:
                    sc.wait_ge(sm["vD"].h, bVD + cj - 1)
                sc.activation(scfb[cj % 2][:, :Gj], scb[cj % 2][:, :Gj],
                              AF.Copy, scale=P126, _lag=None)
                sc.mark("aD")

            emit_scf(0)
            for ci, (kind, c, W, G) in enumerate(chunks):
                g0, ng = grp_of_chunk[ci]
                if ci + 1 < NCH:
                    emit_scf(ci + 1)
                for g in range(ng):
                    if ci >= 2 and g == 0:
                        sc.wait_ge(dSP[ci % 2], bSP[ci % 2] + 16 * (ci // 2))
                    sc.wait_ge(sm["tKV"].h, bTT + g0 + g + 1)
                    sc.activation(tsp[ci % 2][:, 4 * g:4 * g + 4, :],
                                  tps_[(g0 + g) % 2][:, :512], AF.Copy,
                                  _lag=None)
                    sc.mark("vK")
            sc.finish()

        @blk.tensor
        def _(te):
            te.wait_ge(sm["dM"].h, 32)
            for ci, (kind, c, W, G) in enumerate(chunks):
                g0, ng = grp_of_chunk[ci]
                te.wait_ge(sm["vD"].h, bVD + ci + 1)
                ppk = (2 * W) // 128
                for g in range(ng):
                    if g0 + g >= 2:
                        te.wait_ge(sm["vK"].h, bVT + g0 + g - 1)
                    for jj in range(4):
                        j = 4 * g + jj
                        piece = dec[ci % 2][:, j // ppk,
                                            128 * (j % ppk):128 * (j % ppk) + 128]
                        ins = te.transpose(
                            tps_[(g0 + g) % 2][:, 128 * jj:128 * (jj + 1)],
                            piece, ident[:])
                    ins.then_inc(sm["tKV"].h, 1)

        @blk.gpsimd
        def _(gp):
            for ci, (kind, c, W, G) in enumerate(chunks):
                g0, ng = grp_of_chunk[ci]
                gp.wait_ge(sm["vK"].h, bVT + g0 + ng)
                if kind == "g":
                    dst = w3g[c // 4, :, :, 128 * (c % 4):128 * (c % 4) + 128]
                    src = tsp[ci % 2][:, :16, :]
                else:
                    dst = w3d[:, :, 128 * c:128 * (c + 1)].rearrange("a p o -> p a o")
                    src = tsp[ci % 2][:, :44, :]
                gp.dma_start(out=dst, in_=src).then_inc(dSP[ci % 2], 16)
            gp.wait_ge(dSP[0], bSP[0] + 16 * ((NCH + 1) // 2))
            gp.wait_ge(dSP[1], bSP[1] + 16 * (NCH // 2))
    sm["dPK0"].v += 32 * ((NCH + 1) // 2)
    sm["dPK1"].v += 32 * (NCH // 2)
    sm["dSP0"].v += 16 * ((NCH + 1) // 2)
    sm["dSP1"].v += 16 * (NCH // 2)
    sm["tKV"].v += gtot
    nc.all_engine_barrier()

    bW, bKV, bVK = sm["dW"].v, sm["tKV"].v, sm["vK"].v
    wk_t, wv_t = ov["wk_t"], ov["wv_t"]
    cosk_t, sink_t = ov["cosk_t"], ov["sink_t"]
    with (
        nc.psum_tensor([128, 512], F32) as kps0,
        nc.psum_tensor([128, 512], F32) as kps1,
        nc.psum_tensor([128, 512], F32) as sps0,
        nc.psum_tensor([128, 512], F32) as sps1,
        nc.psum_tensor([128, 512], F32) as vps0,
        nc.psum_tensor([128, 512], F32) as vps1,
        nc.Block() as blk,
    ):
        xb = [ov["xb0"], ov["xb1"]]
        kps, sps, vps = [kps0, kps1], [sps0, sps1], [vps0, vps1]

        dX = [sm["dX0"].h, sm["dX1"].h]
        bXp = [sm["dX0"].v, sm["dX1"].v]

        @blk.sync
        def _(sy):
            sy.dma_start(out=cosk_t[:], in_=cosk_in[:, :]).then_inc(sm["dKT"].h, 16)
            sy.dma_start(out=sink_t[:], in_=sink_in[:, :]).then_inc(sm["dKT"].h, 16)

        @blk.gpsimd
        def _(gp):
            gp.dma_start(out=wk_t[:], in_=wk_in[:, :, :]).then_inc(sm["dW"].h, 16)
            gp.dma_start(out=wv_t[:], in_=wv_in[:, :, :]).then_inc(sm["dW"].h, 16)
            for sp in range(8):
                if sp >= 2:
                    gp.wait_ge(sm["tKV"].h, bKV + 12 * (sp - 1))
                gp.dma_start(out=xb[sp % 2][:], in_=x_in[sp]).then_inc(dX[sp % 2], 16)

        @blk.tensor
        def _(te):
            te.wait_ge(sm["dW"].h, bW + 32)
            te.wait_ge(sm["dM"].h, 32)
            for sp in range(8):
                te.wait_ge(dX[sp % 2], bXp[sp % 2] + 16 * (sp // 2 + 1))
                for kvh in range(KVH):
                    pos = sp * 4 + kvh
                    if pos >= 2:
                        spp, kvp = divmod(pos - 2, 4)
                        te.wait_ge(sm["vK"].h, bVK + spp * 12 + 2 * kvp + 2)
                    for c in range(16):
                        ins = te.matmul(kps[kvh % 2][:], wk_t[:, c, 128 * kvh:128 * (kvh + 1)],
                                        xb[sp % 2][:, c, :], start=(c == 0), stop=(c == 15))
                    ins.then_inc(sm["tKV"].h, 1)
                    te.wait_ge(sm["vK"].h, bVK + sp * 12 + 2 * kvh + 1)
                    te.matmul(sps[kvh % 2][:], pswap[:], qtmp[:],
                              start=True, stop=True).then_inc(sm["tKV"].h, 1)
                for tc in range(4):
                    pos = sp * 4 + tc
                    if pos >= 2:
                        spp, tcp = divmod(pos - 2, 4)
                        te.wait_ge(sm["vK"].h, bVK + spp * 12 + 8 + tcp + 1)
                    for c in range(16):
                        ins = te.matmul(vps[tc % 2][:], xb[sp % 2][:, c, 128 * tc:128 * (tc + 1)],
                                        wv_t[:, c, :], start=(c == 0), stop=(c == 15))
                    ins.then_inc(sm["tKV"].h, 1)

        @blk.vector
        def _(v):
            v = Chain(v, "vCH")
            v.wait_ge(sm["dKT"].h, 32)
            for sp in range(8):
                bt = bKV + sp * 12
                for kvh in range(KVH):
                    v.wait_ge(sm["tKV"].h, bt + 2 * kvh + 1)
                    v.tensor_copy(qtmp[:], kps[kvh % 2][:], _lag=None)
                    v.mark("vK")
                    v.wait_ge(sm["tKV"].h, bt + 2 * kvh + 2)
                    cs = cosk_t[:, 512 * sp:512 * (sp + 1)]
                    sn = sink_t[:, 512 * sp:512 * (sp + 1)]
                    v.tensor_mul(t1f[:], sps[kvh % 2][:], sn, _lag=2)
                    v.tensor_mul(t2f[:], kps[kvh % 2][:], cs, _lag=3)
                    v.tensor_add(K_t[:, kvh, 512 * sp:512 * (sp + 1)], t1f[:], t2f[:]
                                 )
                    v.mark("vK")
                for tc in range(4):
                    v.wait_ge(sm["tKV"].h, bt + 8 + tc + 1)
                    v.tensor_copy(V_t[:, 4 * sp + tc, :], vps[tc % 2][:], _lag=None)
                    v.mark("vK")
            v.finish()
    sm["dKT"].v += 32
    sm["dW"].v += 32
    sm["dX0"].v += 16 * 4
    sm["dX1"].v += 16 * 4
    sm["tKV"].v += 12 * 8
    nc.all_engine_barrier()

    for s in range(NSLOT):
        cnt, FF = SLOT_COUNTS[s], SLOT_F[s]

        bXO, bM = sm["dXO"].v, sm["dM"].v
        bOUT = sm["dOUT"].v
        bWq = [sm["dWq0"].v, sm["dWq1"].v]
        dWq = [sm["dWq0"].h, sm["dWq1"].h]
        bTQ, bVQ = sm["tQ"].v, sm["vQ"].v
        with (
            nc.psum_tensor([128, 512], F32) as qps0,
            nc.psum_tensor([128, 512], F32) as qps1,
            nc.psum_tensor([128, 512], F32) as qsp0,
            nc.psum_tensor([128, 512], F32) as qsp1,
            nc.Block() as blk,
        ):
            qps, qsp = [qps0, qps1], [qsp0, qsp1]

            @blk.sync
            def _(sy):
                sy.wait_ge(sm["dOUT"].h, bOUT)
                sy.dma_start(out=masks_t[:], in_=masks_in[s]).then_inc(sm["dM"].h, 16)
                sy.dma_start(out=cosq_t[:], in_=cosq_in[s]).then_inc(sm["dM"].h, 16)
                sy.dma_start(out=sinq_t[:], in_=sinq_in[s]).then_inc(sm["dM"].h, 16)

            @blk.gpsimd
            def _(gp):
                gp.wait_ge(sm["dOUT"].h, bOUT)
                for qh in range(NH):
                    if qh >= 2:
                        gp.wait_ge(sm["tQ"].h,
                                   bTQ + (1 if qh == 2 else 2 * (qh - 2)))
                    gp.dma_start(out=wblk[qh % 2][:], in_=wq_in[qh]
                                 ).then_inc(dWq[qh % 2], 16)

            @blk.vector
            def _(v):
                v = Chain(v, "vCH")
                v.wait_ge(sm["dOUT"].h, bOUT)
                v.wait_ge(sm["dM"].h, bM + 48)
                for qh in range(NH):
                    v.wait_ge(sm["tQ"].h, bTQ + (1 if qh == 0 else 2 * qh))
                    v.tensor_copy(qtmp[:], qps[qh % 2][:], _lag=None)
                    v.mark("vQ")
                    v.wait_ge(sm["tQ"].h,
                              bTQ + (2 * qh + 3 if qh <= NH - 2 else 2 * NH))
                    v.tensor_mul(t1f[:], qsp[qh % 2][:], sinq_t[:], _lag=2)
                    v.tensor_mul(t2f[:], qps[qh % 2][:], cosq_t[:], _lag=3)
                    v.tensor_add(Q_t[:, qh, :], t1f[:], t2f[:])
                    v.mark("vQ")
                v.finish()

            @blk.tensor
            def _(te):
                te.wait_ge(sm["dXO"].h, 16 * (s + 1))

                def emit_swap(qh):
                    te.wait_ge(sm["vQ"].h, bVQ + 2 * qh + 1)
                    te.matmul(qsp[qh % 2][:], pswap[:], qtmp[:],
                              start=True, stop=True).then_inc(sm["tQ"].h, 1)

                for qh in range(NH):
                    te.wait_ge(dWq[qh % 2], bWq[qh % 2] + 16 * (qh // 2 + 1))
                    if qh >= 2:
                        te.wait_ge(sm["vQ"].h, bVQ + 2 * (qh - 2) + 2)
                    for c in range(16):
                        ins = te.matmul(qps[qh % 2][:], wblk[qh % 2][:, c, :],
                                        xob16[:, c, :], start=(c == 0), stop=(c == 15))
                    ins.then_inc(sm["tQ"].h, 1)
                    if qh >= 1:
                        emit_swap(qh - 1)
                emit_swap(NH - 1)
        sm["dM"].v += 48
        sm["dWq0"].v += 16 * (NH // 2)
        sm["dWq1"].v += 16 * (NH // 2)
        sm["tQ"].v += 2 * NH
        nc.all_engine_barrier()

        bS, bAV, bAE, bVA, bVE = (sm["tS"].v, sm["tAV"].v, sm["aE"].v,
                                  sm["vA"].v, sm["vE"].v)
        bAS_A = sm["aS"].v
        PP = (cnt + 1) // 2
        nmask_before = {}
        nm = 0
        for hi in range(NH):
            for kt in range(cnt):
                nmask_before[(hi, kt)] = nm
                if kt >= FF:
                    nm += 1

        def tav_after_pair(gp):
            hi2, p2 = divmod(gp, PP)
            lastkt = min(2 * p2 + 1, cnt - 1)
            return bAV + hi2 * (cnt + 1) + lastkt + 1

        with (
            nc.psum_tensor([128, 2 * SPW], F32) as spp0_,
            nc.psum_tensor([128, 2 * SPW], F32) as spp1_,
            nc.psum_tensor([1, 512], F32) as lps,
            nc.psum_tensor([128, 512], F32) as ops0,
            nc.psum_tensor([128, 512], F32) as ops1,
            nc.psum_tensor([128, 512], F32) as bcp,
            nc.Block() as blk,
        ):
            spp = [spp0_, spp1_]
            opsl = [ops0, ops1]

            @blk.tensor
            def _(te):
                def emit_av(hi, p):
                    gp = hi * PP + p
                    kvh = hi // 4
                    kts = list(range(2 * p, min(2 * p + 2, cnt)))
                    for kt in kts:
                        if kt == 0 and hi >= 1:
                            te.wait_ge(sm["vE"].h, bVE + 3 * hi - 2)
                        if kt >= FF:
                            te.wait_ge(sm["vA"].h, bVA + nmask_before[(hi, kt)] + 1)
                        else:
                            te.wait_ge(sm["aE"].h, bAE + gp + 1)
                        ph = P_t[gp % 2][:, 512 * (kt % 2):512 * (kt % 2) + 512]
                        te.matmul(opsl[hi % 2][:], V_t[:, kt, 128 * kvh:128 * (kvh + 1)],
                                  ph, start=(kt == 0), stop=(kt == cnt - 1))
                    for kt in kts:
                        ph = P_t[gp % 2][:, 512 * (kt % 2):512 * (kt % 2) + 512]
                        te.matmul(lps[:], ones_b[:], ph,
                                  start=(kt == 0), stop=(kt == cnt - 1)
                                  ).then_inc(sm["tAV"].h, 1)

                def emit_bcast(j):
                    te.wait_ge(sm["aS"].h, bAS_A + j + 1)
                    te.matmul(bcp[:], ones_k1[:], lrec[0:1, :], start=True, stop=True
                              ).then_inc(sm["tAV"].h, 1)

                for hi in range(NH):
                    kvh = hi // 4
                    for p in range(PP):
                        gp = hi * PP + p
                        if gp >= 2:
                            te.wait_ge(sm["aE"].h, bAE + gp - 1)
                        for kt in range(2 * p, min(2 * p + 2, cnt)):
                            te.matmul(spp[gp % 2][:, 512 * (kt % 2):512 * (kt % 2) + 512],
                                      K_t[:, kvh, 128 * kt:128 * (kt + 1)],
                                      Q_t[:, hi, :], start=True, stop=True
                                      ).then_inc(sm["tS"].h, 1)
                        if p == 0 and hi >= 1:
                            emit_bcast(hi - 1)
                        if p >= 1:
                            emit_av(hi, p - 1)
                    emit_av(hi, PP - 1)
                emit_bcast(NH - 1)

            @blk.scalar
            def _(sc):
                sc = Chain(sc, "aCH")
                for hi in range(NH):
                    for p in range(PP):
                        gp = hi * PP + p
                        k1 = min(2 * p + 1, cnt - 1)
                        sc.wait_ge(sm["tS"].h, bS + hi * cnt + k1 + 1)
                        if gp >= 2:
                            sc.wait_ge(sm["tAV"].h, tav_after_pair(gp - 2))
                        width = 1024 if 2 * p + 1 < cnt else 512
                        sc.activation(P_t[gp % 2][:, :width], spp[gp % 2][:, :width],
                                      AF.Exp, bias=cb4[:, 0:1], scale=SCALE,
                                      _lag=None)
                        sc.mark("aE")
                        if p == 0 and hi >= 1:
                            sc.wait_ge(sm["vE"].h, bVE + 3 * (hi - 1) + 1)
                            sc.activation(lln[0:1, :], lrow[0:1, :], AF.Ln,
                                          _lag=None)
                            sc.activation(lrec[0:1, :], lln[0:1, :], AF.Exp,
                                          scale=-1.0, _lag=1)
                            sc.mark("aS")
                sc.wait_ge(sm["vE"].h, bVE + 3 * (NH - 1) + 1)
                sc.activation(lln[0:1, :], lrow[0:1, :], AF.Ln, _lag=None)
                sc.activation(lrec[0:1, :], lln[0:1, :], AF.Exp, scale=-1.0, _lag=1)
                sc.mark("aS")
                sc.finish()

            @blk.vector
            def _(v):
                v = Chain(v, "vCH")
                for hi in range(NH):
                    for kt in range(cnt):
                        if kt >= FF:
                            gp = hi * PP + kt // 2
                            v.wait_ge(sm["aE"].h, bAE + gp + 1)
                            mi = kt - FF
                            ph = P_t[gp % 2][:, 512 * (kt % 2):512 * (kt % 2) + 512]
                            v.tensor_mul(ph, ph, masks_t[:, mi, :], _lag=None)
                            v.mark("vA")
                    v.wait_ge(sm["tAV"].h, bAV + hi * (cnt + 1) + cnt)
                    if hi >= 1:
                        v.wait_ge(sm["aS"].h, bAS_A + hi)
                    v.tensor_copy(lrow[0:1, :], lps[0:1, :], _lag=None)
                    v.mark("vE")
                    v.wait_ge(sm["tAV"].h, bAV + (hi + 1) * (cnt + 1))
                    v.tensor_copy(bc0[:], bcp[:], _lag=2)
                    v.mark("vE")
                    v.tensor_mul(On_t[:, hi, :], opsl[hi % 2][:], bc0[:], _lag=1)
                    v.mark("vE")
                v.finish()
        sm["tS"].v += NH * cnt
        sm["tAV"].v += NH * (cnt + 1)
        nc.all_engine_barrier()

        bWq = [sm["dWq0"].v, sm["dWq1"].v]
        dWq = [sm["dWq0"].h, sm["dWq1"].h]
        bWo, bVO, bAS = sm["tWo"].v, sm["vO"].v, sm["aS"].v
        bXO2 = sm["dXF"].v
        with (
            nc.psum_tensor([128, 512], F32) as aps0,
            nc.psum_tensor([128, 512], F32) as aps1,
            nc.psum_tensor([1, 512], F32) as sum_ps,
            nc.psum_tensor([1, 512], F32) as ssq_ps,
            nc.psum_tensor([128, 512], F32) as mub_ps,
            nc.psum_tensor([128, 512], F32) as rsb_ps,
            nc.Block() as blk,
        ):
            aps = [aps0, aps1]

            @blk.gpsimd
            def _(gp):
                for dc in range(16):
                    if dc >= 2:
                        gp.wait_ge(sm["tWo"].h,
                                   bWo + (1 if dc == 2 else 2 * (dc - 2)))
                    gp.dma_start(out=wblk[dc % 2][:], in_=wo_in[dc]
                                 ).then_inc(dWq[dc % 2], 16)
                if s < NSLOT - 1:
                    gp.dma_start(out=xob16[:], in_=xo_in[s + 1]
                                 ).then_inc(sm["dXO"].h, 16)

            @blk.tensor
            def _(te):
                def emit_sums(dc):
                    te.wait_ge(sm["vO"].h, bVO + dc + 1)
                    te.matmul(sum_ps[:], ones_f[:], xof32[:, dc, :],
                              start=(dc == 0), stop=(dc == 15))
                    te.wait_ge(sm["aS"].h, bAS + dc + 1)
                    te.matmul(ssq_ps[:], ones_f[:], sqs[dc % 2][:],
                              start=(dc == 0), stop=(dc == 15)
                              ).then_inc(sm["tWo"].h, 1)

                for dc in range(16):
                    te.wait_ge(dWq[dc % 2], bWq[dc % 2] + 16 * (dc // 2 + 1))
                    if dc >= 2:
                        te.wait_ge(sm["vO"].h, bVO + (dc - 1))
                    for qh in range(NH):
                        ins = te.matmul(aps[dc % 2][:], wblk[dc % 2][:, qh, :],
                                        On_t[:, qh, :], start=(qh == 0), stop=(qh == 15))
                    ins.then_inc(sm["tWo"].h, 1)
                    if dc >= 1:
                        emit_sums(dc - 1)
                emit_sums(15)
                te.wait_ge(sm["vO"].h, bVO + 16 + 2)
                te.matmul(mub_ps[:], ones_k1[:], r0[0:1, :], start=True, stop=True)
                te.wait_ge(sm["aS"].h, bAS + 18)
                te.matmul(rsb_ps[:], ones_k1[:], r1[0:1, :], start=True, stop=True
                          ).then_inc(sm["tWo"].h, 1)

            @blk.scalar
            def _(sc):
                sc = Chain(sc, "aCH")
                sc.dma_start(out=xof32[:], in_=xo_in[s]).then_inc(sm["dXF"].h, 16)
                for dc in range(16):
                    sc.wait_ge(sm["vO"].h, bVO + dc + 1)
                    if dc >= 2:
                        sc.wait_ge(sm["tWo"].h, bWo + 2 * dc - 1)
                    sc.activation(sqs[dc % 2][:], xof32[:, dc, :], AF.Square,
                                  _lag=None)
                    sc.mark("aS")
                sc.wait_ge(sm["vO"].h, bVO + 16 + 2)
                sc.activation(r2[0:1, :], r1[0:1, :], AF.Ln, bias=ceps1[0:1, 0:1],
                              _lag=None)
                sc.mark("aS")
                sc.activation(r1[0:1, :], r2[0:1, :], AF.Exp, scale=-0.5, _lag=1)
                sc.mark("aS")
                sc.finish()

            @blk.vector
            def _(v):
                v = Chain(v, "vCH")
                v.wait_ge(sm["dXF"].h, bXO2 + 16)
                for dc in range(16):
                    v.wait_ge(sm["tWo"].h, bWo + (1 if dc == 0 else 2 * dc))
                    v.tensor_add(xof32[:, dc, :], aps[dc % 2][:], xof32[:, dc, :],
                                 _lag=None)
                    v.mark("vO")
                v.wait_ge(sm["tWo"].h, bWo + 32)
                v.tensor_scalar_mul(r0[0:1, :], sum_ps[0:1, :], INV_D, _lag=None)
                v.tensor_scalar_mul(r1[0:1, :], ssq_ps[0:1, :], INV_D, _lag=None
                                    )
                v.mark("vO")
                v.tensor_mul(r2[0:1, :], r0[0:1, :], r0[0:1, :], _lag=2)
                v.tensor_sub(r1[0:1, :], r1[0:1, :], r2[0:1, :], _lag=1
                             )
                v.mark("vO")
                v.wait_ge(sm["tWo"].h, bWo + 33)
                v.tensor_copy(bc0[:], mub_ps[:], _lag=None)
                v.tensor_copy(bc1[:], rsb_ps[:], _lag=None)
                for dc in range(0, 16, 2):
                    v.tensor_sub(xof32[:, dc, :], xof32[:, dc, :], bc0[:], _lag=2)
                    v.tensor_sub(xof32[:, dc + 1, :], xof32[:, dc + 1, :], bc0[:],
                                 _lag=2)
                    v.tensor_mul(xof32[:, dc, :], xof32[:, dc, :], bc1[:], _lag=2)
                    v.tensor_mul(xof32[:, dc + 1, :], xof32[:, dc + 1, :], bc1[:],
                                 _lag=2)
                    v.tensor_copy(x2b[:, dc, :], xof32[:, dc, :], _lag=2)
                    v.mark("vO")
                    v.tensor_copy(x2b[:, dc + 1, :], xof32[:, dc + 1, :], _lag=2)
                    v.mark("vO")
                v.finish()
        sm["dWq0"].v += 16 * 8
        sm["dWq1"].v += 16 * 8
        sm["dXF"].v += 16
        if s < NSLOT - 1:
            sm["dXO"].v += 16
        sm["tWo"].v += 33

        bRL = [sm["dRL0"].v, sm["dRL1"].v]
        dRL = [sm["dRL0"].h, sm["dRL1"].h]
        bGS = [sm["dGS0"].v, sm["dGS1"].v]
        dGS = [sm["dGS0"].h, sm["dGS1"].h]
        bTF, bAS, bVF = sm["tF"].v, sm["aS"].v, sm["vF"].v
        bVO_pre = sm["vO"].v - 34
        with (
            nc.psum_tensor([128, 512], F32) as gps0,
            nc.psum_tensor([128, 512], F32) as gps1,
            nc.psum_tensor([128, 512], F32) as ups0,
            nc.psum_tensor([128, 512], F32) as ups1,
            nc.Block() as blk,
        ):
            gps, ups = [gps0, gps1], [ups0, ups1]

            @blk.sync
            def _(sy):
                for spo in range(22):
                    if spo == 1:
                        sy.wait_ge(sm["vO"].h, bVO_pre + 34)
                    elif spo >= 2:
                        sy.wait_ge(sm["tF"].h, bTF + 2 * (spo - 1))
                    sy.dma_start(out=gslab[spo % 2][:], in_=w3g[spo, :, :, :]
                                 ).then_inc(dRL[spo % 2], 16)

            @blk.tensor
            def _(te):
                for spo in range(22):
                    te.wait_ge(dRL[spo % 2], bRL[spo % 2] + 16 * (spo // 2 + 1))
                    for jj in range(2):
                        pj = 2 * spo + jj
                        if pj >= 2:
                            te.wait_ge(sm["vF"].h, bVF + pj - 1)
                        for c in range(16):
                            if spo == 0 and jj == 0:
                                te.wait_ge(sm["vO"].h, bVO_pre + 19 + c)
                            te.matmul(gps[jj][:], gslab[spo % 2][:, c, 256 * jj:256 * jj + 128],
                                      x2b[:, c, :], start=(c == 0), stop=(c == 15))
                        for c in range(16):
                            ins = te.matmul(ups[jj][:], gslab[spo % 2][:, c, 256 * jj + 128:256 * jj + 256],
                                            x2b[:, c, :], start=(c == 0), stop=(c == 15))
                        ins.then_inc(sm["tF"].h, 1)

            @blk.scalar
            def _(sc):
                sc = Chain(sc, "aCH")

                def emit_silus(sp2):
                    for jj in range(2):
                        pj = 2 * sp2 + jj
                        sc.wait_ge(sm["tF"].h, bTF + pj + 1)
                        if pj >= 2:
                            sc.wait_ge(sm["vF"].h, bVF + pj - 1)
                        sc.activation(silu_sb[pj % 2][:], gps[jj][:], AF.Silu,
                                      _lag=None)
                        sc.mark("aS")

                for sp2 in range(22):
                    emit_silus(sp2)
                sc.finish()

            @blk.vector
            def _(v):
                v = Chain(v, "vCH")
                for spo in range(22):
                    for jj in range(2):
                        pj = 2 * spo + jj
                        v.wait_ge(sm["aS"].h, bAS + pj + 1)
                        v.tensor_mul(h_t[:, pj, :], silu_sb[pj % 2][:], ups[jj][:],
                                     _lag=None)
                        v.mark("vF")
                v.finish()
        sm["dRL0"].v += 16 * 11
        sm["dRL1"].v += 16 * 11
        sm["tF"].v += 44
        nc.all_engine_barrier()

        bRL2 = [sm["dRL20"].v, sm["dRL21"].v]
        dRL2 = [sm["dRL20"].h, sm["dRL21"].h]
        bTF, bVY = sm["tF"].v, sm["vY"].v
        with (
            nc.psum_tensor([128, 512], F32) as yp0,
            nc.psum_tensor([128, 512], F32) as yp1,
            nc.psum_tensor([128, 512], F32) as yp2,
            nc.psum_tensor([128, 512], F32) as yp3,
            nc.psum_tensor([128, 512], F32) as yp4,
            nc.psum_tensor([128, 512], F32) as yp5,
            nc.psum_tensor([128, 512], F32) as yp6,
            nc.psum_tensor([128, 512], F32) as yp7,
            nc.Block() as blk,
        ):
            yps = [yp0, yp1, yp2, yp3, yp4, yp5, yp6, yp7]

            @blk.sync
            def _(sy):
                for grp in range(2):
                    for jc in range(44):
                        gj = grp * 44 + jc
                        if gj >= 2:
                            sy.wait_ge(sm["tF"].h, bTF + gj - 1)
                        sy.dma_start(out=dslab[gj % 2][:],
                                     in_=w3d[jc][:, 1024 * grp:1024 * (grp + 1)]
                                     ).then_inc(dRL2[gj % 2], 16)

            @blk.tensor
            def _(te):
                for grp in range(2):
                    if grp == 1:
                        te.wait_ge(sm["vY"].h, bVY + 8)
                    for jc in range(44):
                        gj = grp * 44 + jc
                        te.wait_ge(dRL2[gj % 2], bRL2[gj % 2] + 16 * (gj // 2 + 1))
                        for k8 in range(8):
                            dc = grp * 8 + k8
                            ins = te.matmul(yps[k8][:], dslab[gj % 2][:, 128 * k8:128 * (k8 + 1)],
                                            h_t[:, jc, :], start=(jc == 0), stop=(jc == 43))
                        ins.then_inc(sm["tF"].h, 1)

            @blk.vector
            def _(v):
                v = Chain(v, "vCH")
                for grp in range(2):
                    v.wait_ge(sm["tF"].h, bTF + 44 * (grp + 1))
                    for k8 in range(8):
                        v.tensor_copy(yf[:, grp * 8 + k8, :], yps[k8][:], _lag=None)
                        v.mark("vY")
                v.finish()
        sm["dRL20"].v += 16 * 44
        sm["dRL21"].v += 16 * 44
        sm["tF"].v += 88

        bTF2, bVY2, bAS2 = sm["tF"].v, sm["vY"].v, sm["aS"].v
        with (
            nc.psum_tensor([1, 512], F32) as sq1_ps,
            nc.psum_tensor([1, 512], F32) as sum2_ps,
            nc.psum_tensor([1, 512], F32) as sq2_ps,
            nc.psum_tensor([128, 512], F32) as b0_ps,
            nc.psum_tensor([128, 512], F32) as b1_ps,
            nc.Block() as blk,
        ):
            @blk.scalar
            def _(sc):
                sc = Chain(sc, "aCH")
                for dc in range(16):
                    sc.wait_ge(sm["vY"].h, bVY2 - 16 + dc + 1)
                    if dc >= 2:
                        sc.wait_ge(sm["tF"].h, bTF2 + dc - 1)
                    sc.activation(sqs[dc % 2][:], yf[:, dc, :], AF.Square,
                                  _lag=None)
                    sc.mark("aS")
                sc.wait_ge(sm["vY"].h, bVY2 + 1)
                sc.activation(r2[0:1, :], r1[0:1, :], AF.Ln, bias=ceps2[0:1, 0:1],
                              _lag=None)
                sc.mark("aS")
                sc.activation(r1[0:1, :], r2[0:1, :], AF.Exp, scale=-0.5, _lag=1)
                sc.mark("aS")
                for dc in range(16):
                    sc.wait_ge(sm["vY"].h, bVY2 + 1 + dc + 1)
                    if dc >= 2:
                        sc.wait_ge(sm["tF"].h, bTF2 + 16 + dc)
                    sc.activation(sqs[dc % 2][:], yf[:, dc, :], AF.Square,
                                  _lag=None)
                    sc.mark("aS")
                sc.wait_ge(sm["vY"].h, bVY2 + 1 + 16 + 2)
                sc.activation(r2[0:1, :], r1[0:1, :], AF.Ln, bias=ceps1[0:1, 0:1],
                              _lag=None)
                sc.mark("aS")
                sc.activation(r1[0:1, :], r2[0:1, :], AF.Exp, scale=-0.5, _lag=1)
                sc.mark("aS")
                sc.finish()

            @blk.tensor
            def _(te):
                te.wait_ge(sm["vY"].h, bVY2)
                for dc in range(16):
                    te.wait_ge(sm["aS"].h, bAS2 + dc + 1)
                    te.matmul(sq1_ps[:], ones_f[:], sqs[dc % 2][:],
                              start=(dc == 0), stop=(dc == 15)).then_inc(sm["tF"].h, 1)
                te.wait_ge(sm["aS"].h, bAS2 + 18)
                te.matmul(b0_ps[:], ones_k1[:], r1[0:1, :], start=True, stop=True
                          ).then_inc(sm["tF"].h, 1)
                for dc in range(16):
                    te.wait_ge(sm["vY"].h, bVY2 + 1 + dc + 1)
                    te.matmul(sum2_ps[:], ones_f[:], yf[:, dc, :],
                              start=(dc == 0), stop=(dc == 15))
                    te.wait_ge(sm["aS"].h, bAS2 + 18 + dc + 1)
                    te.matmul(sq2_ps[:], ones_f[:], sqs[dc % 2][:],
                              start=(dc == 0), stop=(dc == 15)).then_inc(sm["tF"].h, 1)
                te.wait_ge(sm["vY"].h, bVY2 + 1 + 16 + 2)
                te.matmul(b0_ps[:], ones_k1[:], r0[0:1, :], start=True, stop=True)
                te.wait_ge(sm["aS"].h, bAS2 + 36)
                te.matmul(b1_ps[:], ones_k1[:], r1[0:1, :], start=True, stop=True
                          ).then_inc(sm["tF"].h, 1)

            @blk.vector
            def _(v):
                v = Chain(v, "vCH")
                v.wait_ge(sm["tF"].h, bTF2 + 16)
                v.tensor_scalar_mul(r1[0:1, :], sq1_ps[0:1, :], INV_D, _lag=None)
                v.mark("vY")
                v.wait_ge(sm["tF"].h, bTF2 + 17)
                v.tensor_copy(bc0[:], b0_ps[:], _lag=None)
                for dc in range(0, 16, 2):
                    v.tensor_mul(xof32[:, dc, :], yf[:, dc, :], bc0[:],
                                 _lag=1 if dc == 0 else 2)
                    v.tensor_mul(xof32[:, dc + 1, :], yf[:, dc + 1, :], bc0[:],
                                 _lag=2)
                    v.tensor_add(yf[:, dc, :], xof32[:, dc, :], x2b[:, dc, :],
                                 _lag=2)
                    v.mark("vY")
                    v.tensor_add(yf[:, dc + 1, :], xof32[:, dc + 1, :],
                                 x2b[:, dc + 1, :], _lag=2)
                    v.mark("vY")
                v.wait_ge(sm["tF"].h, bTF2 + 33)
                v.tensor_scalar_mul(r0[0:1, :], sum2_ps[0:1, :], INV_D, _lag=None)
                v.tensor_scalar_mul(r1[0:1, :], sq2_ps[0:1, :], INV_D, _lag=None)
                v.mark("vY")
                v.tensor_mul(r2[0:1, :], r0[0:1, :], r0[0:1, :], _lag=2)
                v.tensor_sub(r1[0:1, :], r1[0:1, :], r2[0:1, :], _lag=1)
                v.mark("vY")
                v.wait_ge(sm["tF"].h, bTF2 + 34)
                v.tensor_copy(bc0[:], b0_ps[:], _lag=None)
                v.tensor_copy(bc1[:], b1_ps[:], _lag=None)
                for dc in range(0, 16, 2):
                    v.tensor_sub(xof32[:, dc, :], yf[:, dc, :], bc0[:], _lag=2)
                    v.tensor_sub(xof32[:, dc + 1, :], yf[:, dc + 1, :], bc0[:],
                                 _lag=2)
                    v.tensor_mul(yf[:, dc, :], xof32[:, dc, :], bc1[:], _lag=2)
                    v.mark("vY")
                    v.tensor_mul(yf[:, dc + 1, :], xof32[:, dc + 1, :], bc1[:],
                                 _lag=2)
                    v.mark("vY")
                v.finish()

            @blk.sync
            def _(sy):
                sy.wait_ge(sm["vY"].h, bVY2 + 35)
                sy.dma_start(out=out_t[:, :, SPW * s:SPW * (s + 1)], in_=yf[:]
                             ).then_inc(sm["dOUT"].h, 16)
        sm["tF"].v += 34
        sm["dOUT"].v += 16
        nc.all_engine_barrier()

    with nc.Block() as blk:
        @blk.sync
        def _(sy):
            sy.wait_ge(sm["dOUT"].h, sm["dOUT"].v)
    return nc


def _rope_tables():
    inv_freq = 1.0 / (10000.0 ** (np.arange(0, DH, 2, dtype=np.float32) / DH))
    freqs = np.arange(S, dtype=np.float32)[:, None] * inv_freq[None, :]
    cos = np.repeat(np.cos(freqs), 2, axis=-1).astype(np.float32).T.copy()
    sin = np.repeat(np.sin(freqs), 2, axis=-1).astype(np.float32).T.copy()
    sin_eff = sin.copy()
    sin_eff[:64] *= -1.0
    return cos, sin_eff


def _mask_tiles(q0s):
    out = np.zeros((NSLOT, 128, 8, SPW), dtype=BF)
    kk = np.arange(128)[:, None]
    qq = np.arange(SPW)[None, :]
    for s_ in range(NSLOT):
        q0 = q0s[s_]
        FF = SLOT_F[s_]
        for m in range(8):
            kt = FF + m
            if kt >= SLOT_COUNTS[s_]:
                continue
            k0 = kt * 128
            off = (k0 - q0) // 128
            if off <= 0:
                out[s_, :, m, :] = 1.0
            else:
                out[s_, :, m, :] = ((k0 + kk) <= (q0 + qq + WIN)).astype(np.float32)
    return out


def kernel(L0_attn_x, L0_attn_Wq, L0_attn_Wk, L0_attn_Wv, L0_attn_Wo,
           L0_ffn_gate_up_packed, L0_ffn_gate_up_scales,
           L0_ffn_down_packed, L0_ffn_down_scales):
    x = np.ascontiguousarray(np.asarray(L0_attn_x, dtype=np.float32))
    wq = np.asarray(L0_attn_Wq, dtype=np.float32)
    wk = np.asarray(L0_attn_Wk, dtype=np.float32)
    wv = np.asarray(L0_attn_Wv, dtype=np.float32)
    wo = np.asarray(L0_attn_Wo, dtype=np.float32)
    gup = np.asarray(L0_ffn_gate_up_packed, dtype=np.int32)
    gus = np.asarray(L0_ffn_gate_up_scales)
    dnp = np.asarray(L0_ffn_down_packed, dtype=np.int32)
    dns = np.asarray(L0_ffn_down_scales)

    cosk, sink = _rope_tables()

    wqT = wq.T[pi_d]
    wq_pack = np.ascontiguousarray(
        wqT.reshape(16, 128, NH, 128).transpose(2, 1, 0, 3))
    wkT = wk.T[pi_d]
    wk_pack = np.ascontiguousarray(wkT.reshape(16, 128, 512).transpose(1, 0, 2))
    wvT = wv.T[pi_d]
    wv_pack = np.ascontiguousarray(wvT.reshape(16, 128, 512).transpose(1, 0, 2))
    woT = wo.T[:, pi_d]
    wo_pack = np.ascontiguousarray(
        woT.reshape(NH, 128, 16, 128).transpose(2, 1, 0, 3))
    gate_p, up_p = gup[:II][pi_j], gup[II:][pi_j]
    gusT = np.asarray(gus, np.float16).T
    gate_s, up_s = gusT[:II][pi_j], gusT[II:][pi_j]
    gup_pack = np.empty((88, 128, 256), np.int32)
    gus_pack = np.empty((88, 128, 16), np.float16)
    for j in range(44):
        gup_pack[2 * j] = gate_p[128 * j:128 * (j + 1)]
        gup_pack[2 * j + 1] = up_p[128 * j:128 * (j + 1)]
        gus_pack[2 * j] = gate_s[128 * j:128 * (j + 1)]
        gus_pack[2 * j + 1] = up_s[128 * j:128 * (j + 1)]
    dnp_pack = np.ascontiguousarray(dnp[pi_d].reshape(16, 128, 704))
    dns_pack = np.ascontiguousarray(
        np.asarray(dns, np.float16).T[pi_d].reshape(16, 128, 44))
    pswap_np = np.zeros((128, 128), dtype=BF)
    for a in range(128):
        pswap_np[(a + 64) % 128, a] = 1.0
    ident_np = np.eye(128, dtype=np.float32).astype(BF)
    cosk_np = np.ascontiguousarray(cosk)
    sink_np = np.ascontiguousarray(sink)


    nc = build_program()

    in_maps = []
    span_sets = [SPANS_A, SPANS_B]
    out_meta = []
    for core in range(8):
        b, half = divmod(core, 2)
        spans = span_sets[half]
        own_cols = np.concatenate([np.arange(sp * SPW, (sp + 1) * SPW) for sp in spans])
        q0s = [sp * SPW for sp in spans]
        xT = x[b].T[pi_d]
        x_pack = np.ascontiguousarray(
            xT.reshape(16, 128, 8, SPW).transpose(2, 1, 0, 3))
        xo = xT[:, own_cols]
        xo_pack = np.ascontiguousarray(
            xo.reshape(16, 128, NSLOT, SPW).transpose(2, 1, 0, 3))
        cosq_np = np.ascontiguousarray(
            cosk[:, own_cols].reshape(128, NSLOT, SPW).transpose(1, 0, 2))
        sinq_np = np.ascontiguousarray(
            sink[:, own_cols].reshape(128, NSLOT, SPW).transpose(1, 0, 2))
        masks_np = _mask_tiles(q0s)
        in_maps.append({
            "x_in": x_pack, "xo_in": xo_pack,
            "wq_in": wq_pack, "wo_in": wo_pack, "wk_in": wk_pack, "wv_in": wv_pack,
            "gup_in": gup_pack, "gus_in": gus_pack,
            "dnp_in": dnp_pack, "dns_in": dns_pack,
            "cosk_in": cosk_np, "sink_in": sink_np,
            "cosq_in": cosq_np, "sinq_in": sinq_np,
            "pswap_in": pswap_np, "masks_in": masks_np, "ident_in": ident_np,
        })
        out_meta.append((b, own_cols))

    res = run_bass_kernel_spmd(nc, in_maps, core_ids=list(range(8)))

    out = np.empty((4, S, D), np.float32)
    for core in range(8):
        b, own_cols = out_meta[core]
        ot = np.asarray(res.results[core]["out_t"])
        operm = ot.transpose(1, 0, 2).reshape(D, 4 * SPW)
        ounp = np.empty_like(operm)
        ounp[pi_d] = operm
        out[b, own_cols, :] = ounp.T
    return out


# version 15 (final)
# speedup vs baseline: 1.0847x; 1.0847x over previous
import numpy as np
import ml_dtypes

import concourse.bass as bass
import concourse.mybir as mybir
from concourse.bass_utils import run_bass_kernel_spmd

F32 = mybir.dt.float32
BF16 = mybir.dt.bfloat16
F16 = mybir.dt.float16
I32 = mybir.dt.int32
AF = mybir.ActivationFunctionType
ALU = mybir.AluOpType
BF = ml_dtypes.bfloat16

S = 4096
D = 2048
NH, KVH, DH = 16, 4, 128
II = 5632
WIN = 128
SPW = 512
NSLOT = 4
SLOT_COUNTS = [9, 17, 25, 32]
SLOT_F = [1, 9, 17, 25]
SPANS_A = [0, 2, 5, 7]
SPANS_B = [1, 3, 4, 6]
SCALE = float(DH ** -0.5)
EXP_BIAS = -4.0
P126 = float(2.0 ** 126)
INV_D = 1.0 / D

pi_d = np.array([4 * (p % 512) + p // 512 for p in range(D)])
pi_j = np.array([4 * (p % 1408) + p // 1408 for p in range(II)])


def build_program():
    nc = bass.Bass("TRN2", target_bir_lowering=False)

    x_in = nc.dram_tensor("x_in", [8, 128, 16, SPW], F32, kind="ExternalInput").ap()
    xo_in = nc.dram_tensor("xo_in", [NSLOT, 128, 16, SPW], F32, kind="ExternalInput").ap()
    wq_in = nc.dram_tensor("wq_in", [NH, 128, 16, 128], F32, kind="ExternalInput").ap()
    wo_in = nc.dram_tensor("wo_in", [16, 128, NH, 128], F32, kind="ExternalInput").ap()
    wk_in = nc.dram_tensor("wk_in", [128, 16, 512], F32, kind="ExternalInput").ap()
    wv_in = nc.dram_tensor("wv_in", [128, 16, 512], F32, kind="ExternalInput").ap()
    gup_in = nc.dram_tensor("gup_in", [88, 128, 256], I32, kind="ExternalInput").ap()
    gus_in = nc.dram_tensor("gus_in", [88, 128, 16], F16, kind="ExternalInput").ap()
    dnp_in = nc.dram_tensor("dnp_in", [16, 128, 704], I32, kind="ExternalInput").ap()
    dns_in = nc.dram_tensor("dns_in", [16, 128, 44], F16, kind="ExternalInput").ap()
    cosk_in = nc.dram_tensor("cosk_in", [128, S], F32, kind="ExternalInput").ap()
    sink_in = nc.dram_tensor("sink_in", [128, S], F32, kind="ExternalInput").ap()
    cosq_in = nc.dram_tensor("cosq_in", [NSLOT, 128, SPW], F32, kind="ExternalInput").ap()
    sinq_in = nc.dram_tensor("sinq_in", [NSLOT, 128, SPW], F32, kind="ExternalInput").ap()
    pswap_in = nc.dram_tensor("pswap_in", [128, 128], BF16, kind="ExternalInput").ap()
    masks_in = nc.dram_tensor("masks_in", [NSLOT, 128, 8, SPW], BF16, kind="ExternalInput").ap()
    w3g = nc.dram_tensor("w3g", [22, 128, 16, 512], BF16, kind="Internal").ap()
    w3d = nc.dram_tensor("w3d", [2, 44, 128, 1024], BF16, kind="Internal").ap()
    ident_in = nc.dram_tensor("ident_in", [128, 128], BF16, kind="ExternalInput").ap()
    out_t = nc.dram_tensor("out_t", [128, 16, 4 * SPW], F32, kind="ExternalOutput").ap()

    class Sem:
        def __init__(self, name):
            self.h = nc.alloc_semaphore(name)
            self.v = 0

    sm = {n: Sem(n) for n in
          ["dPK0", "dPK1", "dX0", "dX1", "dXO", "dXF", "dW", "dWq0", "dWq1",
           "dM", "dKT", "dSP0", "dSP1", "dRL0", "dRL1", "dRL20", "dRL21",
           "dGS0", "dGS1", "dRLg2", "dRL22",
           "dOUT",
           "vD", "vK", "vQ", "vA", "vE", "vO", "vF", "vY", "aE", "aS", "aD",
           "tS", "tAV", "tKV", "tQ", "tWo", "tF", "vCH", "aCH"]}

    _CHAINED = {"tensor_tensor", "tensor_scalar", "tensor_single_scalar",
                "scalar_tensor_tensor", "tensor_copy", "tensor_add",
                "tensor_sub", "tensor_mul", "tensor_max", "tensor_scalar_mul",
                "tensor_scalar_add", "memset", "reciprocal", "activation",
                "mul", "add", "copy", "square", "sqrt"}

    import collections

    class Chain:

        _hist = {}

        def __init__(self, eng, sem_name):
            self._e = eng
            self._s = sm[sem_name]
            self._name = sem_name
            self._pending = None
            self._h = Chain._hist.setdefault(sem_name,
                                             collections.deque(maxlen=16))

        def _flush(self):
            if self._pending is not None:
                self._pending.then_inc(self._s.h, 1)
                self._s.v += 1
                self._h.append((self._s.h, self._s.v))
                self._pending = None

        def mark(self, sem_name, inc=1):
            assert self._pending is not None
            ms = sm[sem_name]
            self._pending.then_inc(ms.h, inc)
            ms.v += inc
            self._h.append((ms.h, ms.v))
            self._pending = None

        def finish(self):
            self._flush()
            if self._h:
                self._e.wait_ge(*self._h[-1])

        def __getattr__(self, name):
            f = getattr(self._e, name)
            if name not in _CHAINED:
                return f

            def wrap(*a, _lag=1, **kw):
                self._flush()
                if _lag is not None and len(self._h) >= _lag:
                    self._e.wait_ge(*self._h[-_lag])
                ins = f(*a, **kw)
                self._pending = ins
                return ins
            return wrap

    K_t = nc.alloc_sbuf_tensor("K_t", [128, KVH, S], BF16)
    V_t = nc.alloc_sbuf_tensor("V_t", [128, 32, 512], BF16)
    pswap = nc.alloc_sbuf_tensor("pswap", [128, 128], BF16)
    ones_b = nc.alloc_sbuf_tensor("ones_b", [128, 1], BF16)
    ones_f = nc.alloc_sbuf_tensor("ones_f", [128, 1], F32)
    ones_k1 = nc.alloc_sbuf_tensor("ones_k1", [1, 128], F32)
    s6 = nc.alloc_sbuf_tensor("s6", [128, 1], I32)
    cb4 = nc.alloc_sbuf_tensor("cb4", [128, 1], F32)
    ceps1 = nc.alloc_sbuf_tensor("ceps1", [128, 1], F32)
    ceps2 = nc.alloc_sbuf_tensor("ceps2", [128, 1], F32)
    xob16 = nc.alloc_sbuf_tensor("xob16", [128, 16, SPW], BF16)
    ident = nc.alloc_sbuf_tensor("ident", [128, 128], BF16)

    DTSZ = {F32: 4, BF16: 2, F16: 2, I32: 4}
    SBUF_TOP = 229344
    overlay_specs = [
        ("pk0", [128, 704], I32, {"W"}), ("pk1", [128, 704], I32, {"W"}),
        ("sc0", [128, 44], F16, {"W"}), ("sc1", [128, 44], F16, {"W"}),
        ("scf0", [128, 1408], BF16, {"W"}), ("scf1", [128, 1408], BF16, {"W"}),
        ("u7t", [128, 4, 704], I32, {"W"}), ("sgt", [128, 4, 704], I32, {"W"}),
        ("bitst", [128, 4, 704], I32, {"W"}),
        ("dec0", [128, 4, 1408], BF16, {"W"}), ("dec1", [128, 4, 1408], BF16, {"W"}),
        ("tsp0", [128, 44, 128], BF16, {"W"}), ("tsp1", [128, 44, 128], BF16, {"W"}),
        ("xb0", [128, 16, 512], BF16, {"P0"}), ("xb1", [128, 16, 512], BF16, {"P0"}),
        ("wk_t", [128, 16, 512], BF16, {"P0"}), ("wv_t", [128, 16, 512], BF16, {"P0"}),
        ("cosk_t", [128, S], F32, {"P0"}), ("sink_t", [128, S], F32, {"P0"}),
        ("masks_t", [128, 8, SPW], BF16, {"Q", "A"}),
        ("cosq_t", [128, SPW], F32, {"Q"}), ("sinq_t", [128, SPW], F32, {"Q"}),
        ("wblk0", [128, 16, 128], BF16, {"Q", "O"}),
        ("wblk1", [128, 16, 128], BF16, {"Q", "O"}),
        ("Q_t", [128, NH, SPW], BF16, {"Q", "A"}),

        ("On_t", [128, NH, SPW], BF16, {"A", "O"}),
        ("xof32", [128, 16, SPW], F32, {"O", "L2"}),
        ("x2b", [128, 16, SPW], BF16, {"O", "Fg", "Fd", "L2"}),
        ("h_t", [128, 44, SPW], BF16, {"Fg", "Fd"}),
        ("gslab0", [128, 16, 512], BF16, {"O", "Fg"}),
        ("gslab1", [128, 16, 512], BF16, {"Fg"}),
        ("gslab2", [128, 16, 512], BF16, {"Fg"}),
        ("dslab0", [128, 1024], BF16, {"Fd"}), ("dslab1", [128, 1024], BF16, {"Fd"}),
        ("dslab2", [128, 1024], BF16, {"Fd"}),
        ("yf", [128, 16, SPW], F32, {"Fd", "L2"}),
        ("lrow", [1, SPW], F32, {"A"}), ("lrec", [1, SPW], F32, {"A"}),
        ("lln", [1, SPW], F32, {"A"}),
        ("r0", [1, SPW], F32, {"O", "L2"}), ("r1", [1, SPW], F32, {"O", "L2"}),
        ("r2", [1, SPW], F32, {"O", "L2"}),
        ("bc0", [128, SPW], F32, {"A", "O", "L2"}),
        ("bc1", [128, SPW], F32, {"O", "L2"}),
        ("P_t0", [128, 2 * SPW], BF16, {"A"}), ("P_t1", [128, 2 * SPW], BF16, {"A"}),
        ("qtmp", [128, SPW], BF16, {"P0", "Q"}),
        ("t1f", [128, SPW], F32, {"P0", "Q"}), ("t2f", [128, SPW], F32, {"P0", "Q"}),
        ("sqs0", [128, SPW], F32, {"O", "L2"}), ("sqs1", [128, SPW], F32, {"O", "L2"}),
        ("silu0", [128, SPW], F32, {"Fg"}), ("silu1", [128, SPW], F32, {"Fg"}),
    ]
    used = []
    placed = {}
    specs = sorted(overlay_specs, key=lambda t: -int(np.prod(t[1][1:])) * DTSZ[t[2]])
    for name, shape, dt_, phases in specs:
        nbytes = int(np.prod(shape[1:])) * DTSZ[dt_]
        nbytes = (nbytes + 31) // 32 * 32
        off = None
        cand = 0
        while True:
            conflict = None
            for (s0_, e0_, ph0) in used:
                if cand < e0_ and cand + nbytes > s0_ and (phases & ph0):
                    conflict = e0_
                    break
            if conflict is None:
                off = cand
                break
            cand = conflict
        used.append((off, off + nbytes, set(phases)))
        placed[name] = (off, shape, dt_)
    ov_size = max(e for _, e, _ in used)
    ov_base = SBUF_TOP - ov_size
    rem = nc.sbuf_bytes_remaining
    assert ov_size + 2048 <= rem, (ov_size, rem)
    ov = {}
    for name, (off, shape, dt_) in placed.items():
        ov[name] = nc.alloc_sbuf_tensor_at(name, shape, dt_, offset=ov_base + off)

    masks_t, cosq_t, sinq_t = ov["masks_t"], ov["cosq_t"], ov["sinq_t"]
    Q_t, On_t = ov["Q_t"], ov["On_t"]
    xof32, x2b, yf, h_t = ov["xof32"], ov["x2b"], ov["yf"], ov["h_t"]
    wblk = [ov["wblk0"], ov["wblk1"]]
    gslab = [ov["gslab0"], ov["gslab1"], ov["gslab2"]]
    dslab = [ov["dslab0"], ov["dslab1"], ov["dslab2"]]
    lrow, lrec, lln = ov["lrow"], ov["lrec"], ov["lln"]
    r0, r1, r2 = ov["r0"], ov["r1"], ov["r2"]
    bc0, bc1 = ov["bc0"], ov["bc1"]
    P_t = [ov["P_t0"], ov["P_t1"]]
    qtmp, t1f, t2f = ov["qtmp"], ov["t1f"], ov["t2f"]
    sqs = [ov["sqs0"], ov["sqs1"]]
    silu_sb = [ov["silu0"], ov["silu1"]]

    with nc.Block() as blk:
        @blk.vector
        def _(v):
            v = Chain(v, "vCH")
            v.memset(s6[:], 6)
            v.memset(cb4[:], EXP_BIAS)
            v.memset(ceps1[:], 1e-5)
            v.memset(ceps2[:], 1e-6)
            v.memset(ones_b[:], 1.0)
            v.memset(ones_f[:], 1.0)
            v.memset(ones_k1[:], 1.0)
            v.finish()

        @blk.sync
        def _(sy):
            sy.dma_start(out=pswap[:], in_=pswap_in[:, :]).then_inc(sm["dM"].h, 16)
            sy.dma_start(out=ident[:], in_=ident_in[:, :]).then_inc(sm["dM"].h, 16)

        @blk.gpsimd
        def _(gp):
            gp.dma_start(out=xob16[:], in_=xo_in[0]).then_inc(sm["dXO"].h, 16)
    sm["dM"].v += 32
    sm["dXO"].v += 16
    nc.all_engine_barrier()

    chunks = [("g", c, 256, 16) for c in range(88)] + [("d", c, 704, 44) for c in range(16)]
    NCH = len(chunks)
    grp_of_chunk = []
    gtot = 0
    for (kind, c, W, G) in chunks:
        nj = (2 * W * 4) // 128 // 4
        grp_of_chunk.append((gtot, nj))
        gtot += nj
    bPK = [sm["dPK0"].v, sm["dPK1"].v]
    bSP = [sm["dSP0"].v, sm["dSP1"].v]
    bVD, bTT, bVT = sm["vD"].v, sm["tKV"].v, sm["vK"].v
    bAD = sm["aD"].v
    dPK = [sm["dPK0"].h, sm["dPK1"].h]
    dSP = [sm["dSP0"].h, sm["dSP1"].h]
    scfb = [ov["scf0"], ov["scf1"]]
    u7t, sgt, bitst = ov["u7t"], ov["sgt"], ov["bitst"]
    with (
        nc.psum_tensor([128, 1024], BF16) as tp0,
        nc.psum_tensor([128, 1024], BF16) as tp1,
        nc.Block() as blk,
    ):
        tps_ = [tp0, tp1]
        pk = [ov["pk0"], ov["pk1"]]
        scb = [ov["sc0"], ov["sc1"]]
        dec = [ov["dec0"], ov["dec1"]]
        tsp = [ov["tsp0"], ov["tsp1"]]

        @blk.sync
        def _(sy):
            for ci, (kind, c, W, G) in enumerate(chunks):
                if ci >= 2:
                    sy.wait_ge(sm["vD"].h, bVD + ci - 1)
                    sy.wait_ge(sm["aD"].h, bAD + ci - 1)
                src_p = gup_in[c] if kind == "g" else dnp_in[c]
                src_s = gus_in[c] if kind == "g" else dns_in[c]
                sy.dma_start(out=pk[ci % 2][:, :W], in_=src_p).then_inc(dPK[ci % 2], 16)
                sy.dma_start(out=scb[ci % 2][:, :G], in_=src_s).then_inc(dPK[ci % 2], 16)

        @blk.vector
        def _(v):
            v = Chain(v, "vCH")
            for ci, (kind, c, W, G) in enumerate(chunks):
                g0, ng = grp_of_chunk[ci]
                v.wait_ge(dPK[ci % 2], bPK[ci % 2] + 32 * (ci // 2 + 1))
                if ci >= 2:
                    v.wait_ge(sm["tKV"].h, bTT + grp_of_chunk[ci - 2][0]
                              + grp_of_chunk[ci - 2][1])
                x = pk[ci % 2][:, :W]
                for k in range(4):
                    sh = 6 - 4 * k
                    v.tensor_scalar(u7t[:, k, :W], x, abs(sh), 0x01C001C0,
                                    ALU.logical_shift_left if sh >= 0
                                    else ALU.logical_shift_right,
                                    ALU.bitwise_and, _lag=4)
                for k in range(4):
                    v.tensor_scalar(sgt[:, k, :W], x, 12 - 4 * k, -2147450880,
                                    ALU.logical_shift_left, ALU.bitwise_and,
                                    _lag=4)
                for k in range(4):
                    v.tensor_tensor(bitst[:, k, :W], u7t[:, k, :W],
                                    sgt[:, k, :W], ALU.bitwise_or, _lag=4)
                v.wait_ge(sm["aD"].h, bAD + ci + 1)
                for k in range(4):
                    ins = v.tensor_tensor(
                        dec[ci % 2][:, k, :2 * W],
                        bitst[:, k, :W].bitcast(BF16),
                        scfb[ci % 2][:, :2 * W],
                        ALU.mult, _lag=4)
                v.mark("vD")
            v.finish()

        @blk.scalar
        def _(sc):
            sc = Chain(sc, "aCH")

            def emit_scf(cj):
                _, _, Wj, Gj = chunks[cj]
                sc.wait_ge(dPK[cj % 2], bPK[cj % 2] + 32 * (cj // 2 + 1))
                if cj >= 2:
                    sc.wait_ge(sm["vD"].h, bVD + cj - 1)
                scbx = scb[cj % 2]
                sc.activation(
                    scfb[cj % 2][:, :2 * Wj].rearrange("p (g j) -> p g j", j=32),
                    bass.AP(scbx.ap().tensor, scbx.ap().offset,
                            [scbx.ap().ap[0], [1, Gj], [0, 32]]),
                    AF.Copy, scale=P126, _lag=None)
                sc.mark("aD")

            emit_scf(0)
            for ci, (kind, c, W, G) in enumerate(chunks):
                g0, ng = grp_of_chunk[ci]
                if ci + 1 < NCH:
                    emit_scf(ci + 1)
                for g in range(ng):
                    if ci >= 2 and g == 0:
                        sc.wait_ge(dSP[ci % 2], bSP[ci % 2] + 16 * (ci // 2))
                    sc.wait_ge(sm["tKV"].h, bTT + g0 + g + 1)
                    sc.activation(tsp[ci % 2][:, 4 * g:4 * g + 4, :],
                                  tps_[(g0 + g) % 2][:, :512], AF.Copy,
                                  _lag=None)
                    sc.mark("vK")
            sc.finish()

        @blk.tensor
        def _(te):
            te.wait_ge(sm["dM"].h, 32)
            for ci, (kind, c, W, G) in enumerate(chunks):
                g0, ng = grp_of_chunk[ci]
                te.wait_ge(sm["vD"].h, bVD + ci + 1)
                ppk = (2 * W) // 128
                for g in range(ng):
                    if g0 + g >= 2:
                        te.wait_ge(sm["vK"].h, bVT + g0 + g - 1)
                    for jj in range(4):
                        j = 4 * g + jj
                        piece = dec[ci % 2][:, j // ppk,
                                            128 * (j % ppk):128 * (j % ppk) + 128]
                        ins = te.transpose(
                            tps_[(g0 + g) % 2][:, 128 * jj:128 * (jj + 1)],
                            piece, ident[:])
                    ins.then_inc(sm["tKV"].h, 1)

        @blk.gpsimd
        def _(gp):
            for ci, (kind, c, W, G) in enumerate(chunks):
                g0, ng = grp_of_chunk[ci]
                gp.wait_ge(sm["vK"].h, bVT + g0 + ng)
                if kind == "g":
                    dst = w3g[c // 4, :, :, 128 * (c % 4):128 * (c % 4) + 128]
                    src = tsp[ci % 2][:, :16, :]
                else:
                    dst = w3d[c // 8, :, :, 128 * (c % 8):128 * (c % 8) + 128
                              ].rearrange("a p o -> p a o")
                    src = tsp[ci % 2][:, :44, :]
                gp.dma_start(out=dst, in_=src).then_inc(dSP[ci % 2], 16)
            gp.wait_ge(dSP[0], bSP[0] + 16 * ((NCH + 1) // 2))
            gp.wait_ge(dSP[1], bSP[1] + 16 * (NCH // 2))
    sm["dPK0"].v += 32 * ((NCH + 1) // 2)
    sm["dPK1"].v += 32 * (NCH // 2)
    sm["dSP0"].v += 16 * ((NCH + 1) // 2)
    sm["dSP1"].v += 16 * (NCH // 2)
    sm["tKV"].v += gtot
    nc.all_engine_barrier()

    bW, bKV, bVK = sm["dW"].v, sm["tKV"].v, sm["vK"].v
    wk_t, wv_t = ov["wk_t"], ov["wv_t"]
    cosk_t, sink_t = ov["cosk_t"], ov["sink_t"]
    with (
        nc.psum_tensor([128, 512], F32) as kps0,
        nc.psum_tensor([128, 512], F32) as kps1,
        nc.psum_tensor([128, 512], F32) as sps0,
        nc.psum_tensor([128, 512], F32) as sps1,
        nc.psum_tensor([128, 512], F32) as vps0,
        nc.psum_tensor([128, 512], F32) as vps1,
        nc.Block() as blk,
    ):
        xb = [ov["xb0"], ov["xb1"]]
        kps, sps, vps = [kps0, kps1], [sps0, sps1], [vps0, vps1]

        dX = [sm["dX0"].h, sm["dX1"].h]
        bXp = [sm["dX0"].v, sm["dX1"].v]

        @blk.sync
        def _(sy):
            sy.dma_start(out=cosk_t[:], in_=cosk_in[:, :]).then_inc(sm["dKT"].h, 16)
            sy.dma_start(out=sink_t[:], in_=sink_in[:, :]).then_inc(sm["dKT"].h, 16)

        @blk.gpsimd
        def _(gp):
            gp.dma_start(out=wk_t[:], in_=wk_in[:, :, :]).then_inc(sm["dW"].h, 16)
            gp.dma_start(out=wv_t[:], in_=wv_in[:, :, :]).then_inc(sm["dW"].h, 16)
            for sp in range(8):
                if sp >= 2:
                    gp.wait_ge(sm["tKV"].h, bKV + 12 * (sp - 1))
                gp.dma_start(out=xb[sp % 2][:], in_=x_in[sp]).then_inc(dX[sp % 2], 16)

        @blk.tensor
        def _(te):
            te.wait_ge(sm["dW"].h, bW + 32)
            te.wait_ge(sm["dM"].h, 32)
            for sp in range(8):
                te.wait_ge(dX[sp % 2], bXp[sp % 2] + 16 * (sp // 2 + 1))
                for kvh in range(KVH):
                    pos = sp * 4 + kvh
                    if pos >= 2:
                        spp, kvp = divmod(pos - 2, 4)
                        te.wait_ge(sm["vK"].h, bVK + spp * 12 + 2 * kvp + 2)
                    for c in range(16):
                        ins = te.matmul(kps[kvh % 2][:], wk_t[:, c, 128 * kvh:128 * (kvh + 1)],
                                        xb[sp % 2][:, c, :], start=(c == 0), stop=(c == 15))
                    ins.then_inc(sm["tKV"].h, 1)
                    te.wait_ge(sm["vK"].h, bVK + sp * 12 + 2 * kvh + 1)
                    te.matmul(sps[kvh % 2][:], pswap[:], qtmp[:],
                              start=True, stop=True).then_inc(sm["tKV"].h, 1)
                for tc in range(4):
                    pos = sp * 4 + tc
                    if pos >= 2:
                        spp, tcp = divmod(pos - 2, 4)
                        te.wait_ge(sm["vK"].h, bVK + spp * 12 + 8 + tcp + 1)
                    for c in range(16):
                        ins = te.matmul(vps[tc % 2][:], xb[sp % 2][:, c, 128 * tc:128 * (tc + 1)],
                                        wv_t[:, c, :], start=(c == 0), stop=(c == 15))
                    ins.then_inc(sm["tKV"].h, 1)

        @blk.vector
        def _(v):
            v = Chain(v, "vCH")
            v.wait_ge(sm["dKT"].h, 32)
            for sp in range(8):
                bt = bKV + sp * 12
                for kvh in range(KVH):
                    v.wait_ge(sm["tKV"].h, bt + 2 * kvh + 1)
                    v.tensor_copy(qtmp[:], kps[kvh % 2][:], _lag=None)
                    v.mark("vK")
                    v.wait_ge(sm["tKV"].h, bt + 2 * kvh + 2)
                    cs = cosk_t[:, 512 * sp:512 * (sp + 1)]
                    sn = sink_t[:, 512 * sp:512 * (sp + 1)]
                    v.tensor_mul(t1f[:], sps[kvh % 2][:], sn, _lag=2)
                    v.tensor_mul(t2f[:], kps[kvh % 2][:], cs, _lag=3)
                    v.tensor_add(K_t[:, kvh, 512 * sp:512 * (sp + 1)], t1f[:], t2f[:]
                                 )
                    v.mark("vK")
                for tc in range(4):
                    v.wait_ge(sm["tKV"].h, bt + 8 + tc + 1)
                    v.tensor_copy(V_t[:, 4 * sp + tc, :], vps[tc % 2][:], _lag=None)
                    v.mark("vK")
            v.finish()
    sm["dKT"].v += 32
    sm["dW"].v += 32
    sm["dX0"].v += 16 * 4
    sm["dX1"].v += 16 * 4
    sm["tKV"].v += 12 * 8
    nc.all_engine_barrier()

    for s in range(NSLOT):
        cnt, FF = SLOT_COUNTS[s], SLOT_F[s]

        bXO, bM = sm["dXO"].v, sm["dM"].v
        bOUT = sm["dOUT"].v
        bWq = [sm["dWq0"].v, sm["dWq1"].v]
        dWq = [sm["dWq0"].h, sm["dWq1"].h]
        bTQ, bVQ = sm["tQ"].v, sm["vQ"].v
        with (
            nc.psum_tensor([128, 512], F32) as qps0,
            nc.psum_tensor([128, 512], F32) as qps1,
            nc.psum_tensor([128, 512], F32) as qsp0,
            nc.psum_tensor([128, 512], F32) as qsp1,
            nc.Block() as blk,
        ):
            qps, qsp = [qps0, qps1], [qsp0, qsp1]

            @blk.sync
            def _(sy):
                sy.wait_ge(sm["dOUT"].h, bOUT)
                sy.dma_start(out=masks_t[:], in_=masks_in[s]).then_inc(sm["dM"].h, 16)
                sy.dma_start(out=cosq_t[:], in_=cosq_in[s]).then_inc(sm["dM"].h, 16)
                sy.dma_start(out=sinq_t[:], in_=sinq_in[s]).then_inc(sm["dM"].h, 16)

            @blk.gpsimd
            def _(gp):
                gp.wait_ge(sm["dOUT"].h, bOUT)
                for qh in range(NH):
                    if qh >= 2:
                        gp.wait_ge(sm["tQ"].h,
                                   bTQ + (1 if qh == 2 else 2 * (qh - 2)))
                    gp.dma_start(out=wblk[qh % 2][:], in_=wq_in[qh]
                                 ).then_inc(dWq[qh % 2], 16)

            @blk.vector
            def _(v):
                v = Chain(v, "vCH")
                v.wait_ge(sm["dOUT"].h, bOUT)
                v.wait_ge(sm["dM"].h, bM + 48)
                for qh in range(NH):
                    v.wait_ge(sm["tQ"].h, bTQ + (1 if qh == 0 else 2 * qh))
                    v.tensor_copy(qtmp[:], qps[qh % 2][:], _lag=None)
                    v.mark("vQ")
                    v.wait_ge(sm["tQ"].h,
                              bTQ + (2 * qh + 3 if qh <= NH - 2 else 2 * NH))
                    v.tensor_mul(t1f[:], qsp[qh % 2][:], sinq_t[:], _lag=2)
                    v.tensor_mul(t2f[:], qps[qh % 2][:], cosq_t[:], _lag=3)
                    v.tensor_add(Q_t[:, qh, :], t1f[:], t2f[:])
                    v.mark("vQ")
                v.finish()

            @blk.tensor
            def _(te):
                te.wait_ge(sm["dXO"].h, 16 * (s + 1))

                def emit_swap(qh):
                    te.wait_ge(sm["vQ"].h, bVQ + 2 * qh + 1)
                    te.matmul(qsp[qh % 2][:], pswap[:], qtmp[:],
                              start=True, stop=True).then_inc(sm["tQ"].h, 1)

                for qh in range(NH):
                    te.wait_ge(dWq[qh % 2], bWq[qh % 2] + 16 * (qh // 2 + 1))
                    if qh >= 2:
                        te.wait_ge(sm["vQ"].h, bVQ + 2 * (qh - 2) + 2)
                    for c in range(16):
                        ins = te.matmul(qps[qh % 2][:], wblk[qh % 2][:, c, :],
                                        xob16[:, c, :], start=(c == 0), stop=(c == 15))
                    ins.then_inc(sm["tQ"].h, 1)
                    if qh >= 1:
                        emit_swap(qh - 1)
                emit_swap(NH - 1)
        sm["dM"].v += 48
        sm["dWq0"].v += 16 * (NH // 2)
        sm["dWq1"].v += 16 * (NH // 2)
        sm["tQ"].v += 2 * NH
        nc.all_engine_barrier()

        bS, bAV, bAE, bVA, bVE = (sm["tS"].v, sm["tAV"].v, sm["aE"].v,
                                  sm["vA"].v, sm["vE"].v)
        bAS_A = sm["aS"].v
        PP = (cnt + 1) // 2
        nmask_before = {}
        nm = 0
        for hi in range(NH):
            for kt in range(cnt):
                nmask_before[(hi, kt)] = nm
                if kt >= FF:
                    nm += 1

        def tav_after_pair(gp):
            hi2, p2 = divmod(gp, PP)
            lastkt = min(2 * p2 + 1, cnt - 1)
            return bAV + hi2 * (cnt + 1) + lastkt + 1

        with (
            nc.psum_tensor([128, 2 * SPW], F32) as spp0_,
            nc.psum_tensor([128, 2 * SPW], F32) as spp1_,
            nc.psum_tensor([1, 512], F32) as lps,
            nc.psum_tensor([128, 512], F32) as ops0,
            nc.psum_tensor([128, 512], F32) as ops1,
            nc.psum_tensor([128, 512], F32) as bcp,
            nc.Block() as blk,
        ):
            spp = [spp0_, spp1_]
            opsl = [ops0, ops1]

            @blk.tensor
            def _(te):
                def emit_av(hi, p):
                    gp = hi * PP + p
                    kvh = hi // 4
                    kts = list(range(2 * p, min(2 * p + 2, cnt)))
                    for kt in kts:
                        if kt == 0 and hi >= 1:
                            te.wait_ge(sm["vE"].h, bVE + 3 * hi - 2)
                        if kt >= FF:
                            te.wait_ge(sm["vA"].h, bVA + nmask_before[(hi, kt)] + 1)
                        else:
                            te.wait_ge(sm["aE"].h, bAE + gp + 1)
                        ph = P_t[gp % 2][:, 512 * (kt % 2):512 * (kt % 2) + 512]
                        te.matmul(opsl[hi % 2][:], V_t[:, kt, 128 * kvh:128 * (kvh + 1)],
                                  ph, start=(kt == 0), stop=(kt == cnt - 1))
                    for kt in kts:
                        ph = P_t[gp % 2][:, 512 * (kt % 2):512 * (kt % 2) + 512]
                        te.matmul(lps[:], ones_b[:], ph,
                                  start=(kt == 0), stop=(kt == cnt - 1)
                                  ).then_inc(sm["tAV"].h, 1)

                def emit_bcast(j):
                    te.wait_ge(sm["aS"].h, bAS_A + j + 1)
                    te.matmul(bcp[:], ones_k1[:], lrec[0:1, :], start=True, stop=True
                              ).then_inc(sm["tAV"].h, 1)

                for hi in range(NH):
                    kvh = hi // 4
                    for p in range(PP):
                        gp = hi * PP + p
                        if gp >= 2:
                            te.wait_ge(sm["aE"].h, bAE + gp - 1)
                        for kt in range(2 * p, min(2 * p + 2, cnt)):
                            te.matmul(spp[gp % 2][:, 512 * (kt % 2):512 * (kt % 2) + 512],
                                      K_t[:, kvh, 128 * kt:128 * (kt + 1)],
                                      Q_t[:, hi, :], start=True, stop=True
                                      ).then_inc(sm["tS"].h, 1)
                        if p == 0 and hi >= 1:
                            emit_bcast(hi - 1)
                        if p >= 1:
                            emit_av(hi, p - 1)
                    emit_av(hi, PP - 1)
                emit_bcast(NH - 1)

            @blk.scalar
            def _(sc):
                sc = Chain(sc, "aCH")
                for hi in range(NH):
                    for p in range(PP):
                        gp = hi * PP + p
                        k1 = min(2 * p + 1, cnt - 1)
                        sc.wait_ge(sm["tS"].h, bS + hi * cnt + k1 + 1)
                        if gp >= 2:
                            sc.wait_ge(sm["tAV"].h, tav_after_pair(gp - 2))
                        width = 1024 if 2 * p + 1 < cnt else 512
                        sc.activation(P_t[gp % 2][:, :width], spp[gp % 2][:, :width],
                                      AF.Exp, bias=cb4[:, 0:1], scale=SCALE,
                                      _lag=None)
                        sc.mark("aE")
                        if p == 0 and hi >= 1:
                            sc.wait_ge(sm["vE"].h, bVE + 3 * (hi - 1) + 1)
                            sc.activation(lln[0:1, :], lrow[0:1, :], AF.Ln,
                                          _lag=None)
                            sc.activation(lrec[0:1, :], lln[0:1, :], AF.Exp,
                                          scale=-1.0, _lag=1)
                            sc.mark("aS")
                sc.wait_ge(sm["vE"].h, bVE + 3 * (NH - 1) + 1)
                sc.activation(lln[0:1, :], lrow[0:1, :], AF.Ln, _lag=None)
                sc.activation(lrec[0:1, :], lln[0:1, :], AF.Exp, scale=-1.0, _lag=1)
                sc.mark("aS")
                sc.finish()

            @blk.vector
            def _(v):
                v = Chain(v, "vCH")
                for hi in range(NH):
                    for kt in range(cnt):
                        if kt >= FF:
                            gp = hi * PP + kt // 2
                            v.wait_ge(sm["aE"].h, bAE + gp + 1)
                            mi = kt - FF
                            ph = P_t[gp % 2][:, 512 * (kt % 2):512 * (kt % 2) + 512]
                            v.tensor_mul(ph, ph, masks_t[:, mi, :], _lag=None)
                            v.mark("vA")
                    v.wait_ge(sm["tAV"].h, bAV + hi * (cnt + 1) + cnt)
                    if hi >= 1:
                        v.wait_ge(sm["aS"].h, bAS_A + hi)
                    v.tensor_copy(lrow[0:1, :], lps[0:1, :], _lag=None)
                    v.mark("vE")
                    v.wait_ge(sm["tAV"].h, bAV + (hi + 1) * (cnt + 1))
                    v.tensor_copy(bc0[:], bcp[:], _lag=2)
                    v.mark("vE")
                    v.tensor_mul(On_t[:, hi, :], opsl[hi % 2][:], bc0[:], _lag=1)
                    v.mark("vE")
                v.finish()
        sm["tS"].v += NH * cnt
        sm["tAV"].v += NH * (cnt + 1)
        nc.all_engine_barrier()

        bWq = [sm["dWq0"].v, sm["dWq1"].v]
        dWq = [sm["dWq0"].h, sm["dWq1"].h]
        bWo, bVO, bAS = sm["tWo"].v, sm["vO"].v, sm["aS"].v
        bXO2 = sm["dXF"].v
        with (
            nc.psum_tensor([128, 512], F32) as aps0,
            nc.psum_tensor([128, 512], F32) as aps1,
            nc.psum_tensor([1, 512], F32) as sum_ps,
            nc.psum_tensor([1, 512], F32) as ssq_ps,
            nc.psum_tensor([128, 512], F32) as mub_ps,
            nc.psum_tensor([128, 512], F32) as rsb_ps,
            nc.Block() as blk,
        ):
            aps = [aps0, aps1]

            @blk.gpsimd
            def _(gp):
                for dc in range(16):
                    if dc >= 2:
                        gp.wait_ge(sm["tWo"].h,
                                   bWo + (1 if dc == 2 else 2 * (dc - 2)))
                    gp.dma_start(out=wblk[dc % 2][:], in_=wo_in[dc]
                                 ).then_inc(dWq[dc % 2], 16)
                if s < NSLOT - 1:
                    gp.dma_start(out=xob16[:], in_=xo_in[s + 1]
                                 ).then_inc(sm["dXO"].h, 16)

            @blk.tensor
            def _(te):
                def emit_sums(dc):
                    te.wait_ge(sm["vO"].h, bVO + dc + 1)
                    te.matmul(sum_ps[:], ones_f[:], xof32[:, dc, :],
                              start=(dc == 0), stop=(dc == 15))
                    te.wait_ge(sm["aS"].h, bAS + dc + 1)
                    te.matmul(ssq_ps[:], ones_f[:], sqs[dc % 2][:],
                              start=(dc == 0), stop=(dc == 15)
                              ).then_inc(sm["tWo"].h, 1)

                for dc in range(16):
                    te.wait_ge(dWq[dc % 2], bWq[dc % 2] + 16 * (dc // 2 + 1))
                    if dc >= 2:
                        te.wait_ge(sm["vO"].h, bVO + (dc - 1))
                    for qh in range(NH):
                        ins = te.matmul(aps[dc % 2][:], wblk[dc % 2][:, qh, :],
                                        On_t[:, qh, :], start=(qh == 0), stop=(qh == 15))
                    ins.then_inc(sm["tWo"].h, 1)
                    if dc >= 1:
                        emit_sums(dc - 1)
                emit_sums(15)
                te.wait_ge(sm["vO"].h, bVO + 16 + 2)
                te.matmul(mub_ps[:], ones_k1[:], r0[0:1, :], start=True, stop=True)
                te.wait_ge(sm["aS"].h, bAS + 18)
                te.matmul(rsb_ps[:], ones_k1[:], r1[0:1, :], start=True, stop=True
                          ).then_inc(sm["tWo"].h, 1)

            @blk.scalar
            def _(sc):
                sc = Chain(sc, "aCH")
                sc.dma_start(out=xof32[:], in_=xo_in[s]).then_inc(sm["dXF"].h, 16)
                for dc in range(16):
                    sc.wait_ge(sm["vO"].h, bVO + dc + 1)
                    if dc >= 2:
                        sc.wait_ge(sm["tWo"].h, bWo + 2 * dc - 1)
                    sc.activation(sqs[dc % 2][:], xof32[:, dc, :], AF.Square,
                                  _lag=None)
                    sc.mark("aS")
                sc.wait_ge(sm["vO"].h, bVO + 16 + 2)
                sc.activation(r2[0:1, :], r1[0:1, :], AF.Ln, bias=ceps1[0:1, 0:1],
                              _lag=None)
                sc.mark("aS")
                sc.activation(r1[0:1, :], r2[0:1, :], AF.Exp, scale=-0.5, _lag=1)
                sc.mark("aS")
                sc.finish()

            @blk.vector
            def _(v):
                v = Chain(v, "vCH")
                v.wait_ge(sm["dXF"].h, bXO2 + 16)
                for dc in range(16):
                    v.wait_ge(sm["tWo"].h, bWo + (1 if dc == 0 else 2 * dc))
                    v.tensor_add(xof32[:, dc, :], aps[dc % 2][:], xof32[:, dc, :],
                                 _lag=None)
                    v.mark("vO")
                v.wait_ge(sm["tWo"].h, bWo + 32)
                v.tensor_scalar_mul(r0[0:1, :], sum_ps[0:1, :], INV_D, _lag=None)
                v.tensor_scalar_mul(r1[0:1, :], ssq_ps[0:1, :], INV_D, _lag=None
                                    )
                v.mark("vO")
                v.tensor_mul(r2[0:1, :], r0[0:1, :], r0[0:1, :], _lag=2)
                v.tensor_sub(r1[0:1, :], r1[0:1, :], r2[0:1, :], _lag=1
                             )
                v.mark("vO")
                v.wait_ge(sm["tWo"].h, bWo + 33)
                v.tensor_copy(bc0[:], mub_ps[:], _lag=None)
                v.tensor_copy(bc1[:], rsb_ps[:], _lag=None)
                for dc in range(0, 16, 2):
                    v.tensor_sub(xof32[:, dc, :], xof32[:, dc, :], bc0[:], _lag=2)
                    v.tensor_sub(xof32[:, dc + 1, :], xof32[:, dc + 1, :], bc0[:],
                                 _lag=2)
                    v.tensor_mul(xof32[:, dc, :], xof32[:, dc, :], bc1[:], _lag=2)
                    v.tensor_mul(xof32[:, dc + 1, :], xof32[:, dc + 1, :], bc1[:],
                                 _lag=2)
                    v.tensor_copy(x2b[:, dc, :], xof32[:, dc, :], _lag=2)
                    v.mark("vO")
                    v.tensor_copy(x2b[:, dc + 1, :], xof32[:, dc + 1, :], _lag=2)
                    v.mark("vO")
                v.finish()
        sm["dWq0"].v += 16 * 8
        sm["dWq1"].v += 16 * 8
        sm["dXF"].v += 16
        if s < NSLOT - 1:
            sm["dXO"].v += 16
        sm["tWo"].v += 33

        bRL = [sm["dRL0"].v, sm["dRL1"].v, sm["dRLg2"].v]
        dRL = [sm["dRL0"].h, sm["dRL1"].h, sm["dRLg2"].h]
        bGS = [sm["dGS0"].v, sm["dGS1"].v]
        dGS = [sm["dGS0"].h, sm["dGS1"].h]
        bTF, bAS, bVF = sm["tF"].v, sm["aS"].v, sm["vF"].v
        bVO_pre = sm["vO"].v - 34
        with (
            nc.psum_tensor([128, 512], F32) as gps0,
            nc.psum_tensor([128, 512], F32) as gps1,
            nc.psum_tensor([128, 512], F32) as ups0,
            nc.psum_tensor([128, 512], F32) as ups1,
            nc.Block() as blk,
        ):
            gps, ups = [gps0, gps1], [ups0, ups1]

            @blk.sync
            def _(sy):
                for spo in range(22):
                    if spo in (1, 2):
                        sy.wait_ge(sm["vO"].h, bVO_pre + 34)
                    elif spo >= 3:
                        sy.wait_ge(sm["tF"].h, bTF + 2 * (spo - 2))
                    sy.dma_start(out=gslab[spo % 3][:], in_=w3g[spo, :, :, :]
                                 ).then_inc(dRL[spo % 3], 16)

            @blk.tensor
            def _(te):
                for spo in range(22):
                    te.wait_ge(dRL[spo % 3], bRL[spo % 3] + 16 * (spo // 3 + 1))
                    for jj in range(2):
                        pj = 2 * spo + jj
                        if pj >= 2:
                            te.wait_ge(sm["vF"].h, bVF + pj - 1)
                        for c in range(16):
                            if spo == 0 and jj == 0:
                                te.wait_ge(sm["vO"].h, bVO_pre + 19 + c)
                            te.matmul(gps[jj][:], gslab[spo % 3][:, c, 256 * jj:256 * jj + 128],
                                      x2b[:, c, :], start=(c == 0), stop=(c == 15))
                        for c in range(16):
                            ins = te.matmul(ups[jj][:], gslab[spo % 3][:, c, 256 * jj + 128:256 * jj + 256],
                                            x2b[:, c, :], start=(c == 0), stop=(c == 15))
                        ins.then_inc(sm["tF"].h, 1)

            @blk.scalar
            def _(sc):
                sc = Chain(sc, "aCH")

                def emit_silus(sp2):
                    for jj in range(2):
                        pj = 2 * sp2 + jj
                        sc.wait_ge(sm["tF"].h, bTF + pj + 1)
                        if pj >= 2:
                            sc.wait_ge(sm["vF"].h, bVF + pj - 1)
                        sc.activation(silu_sb[pj % 2][:], gps[jj][:], AF.Silu,
                                      _lag=None)
                        sc.mark("aS")

                for sp2 in range(22):
                    emit_silus(sp2)
                sc.finish()

            @blk.vector
            def _(v):
                v = Chain(v, "vCH")
                for spo in range(22):
                    for jj in range(2):
                        pj = 2 * spo + jj
                        v.wait_ge(sm["aS"].h, bAS + pj + 1)
                        v.tensor_mul(h_t[:, pj, :], silu_sb[pj % 2][:], ups[jj][:],
                                     _lag=None)
                        v.mark("vF")
                v.finish()
        sm["dRL0"].v += 16 * 8
        sm["dRL1"].v += 16 * 7
        sm["dRLg2"].v += 16 * 7
        sm["tF"].v += 44
        nc.all_engine_barrier()

        bRL2 = [sm["dRL20"].v, sm["dRL21"].v, sm["dRL22"].v]
        dRL2 = [sm["dRL20"].h, sm["dRL21"].h, sm["dRL22"].h]
        bTF, bVY = sm["tF"].v, sm["vY"].v
        with (
            nc.psum_tensor([128, 512], F32) as yp0,
            nc.psum_tensor([128, 512], F32) as yp1,
            nc.psum_tensor([128, 512], F32) as yp2,
            nc.psum_tensor([128, 512], F32) as yp3,
            nc.psum_tensor([128, 512], F32) as yp4,
            nc.psum_tensor([128, 512], F32) as yp5,
            nc.psum_tensor([128, 512], F32) as yp6,
            nc.psum_tensor([128, 512], F32) as yp7,
            nc.Block() as blk,
        ):
            yps = [yp0, yp1, yp2, yp3, yp4, yp5, yp6, yp7]

            @blk.sync
            def _(sy):
                for grp in range(2):
                    for jc in range(44):
                        gj = grp * 44 + jc
                        if gj >= 3:
                            sy.wait_ge(sm["tF"].h, bTF + gj - 2)
                        sy.dma_start(out=dslab[gj % 3][:],
                                     in_=w3d[grp, jc]
                                     ).then_inc(dRL2[gj % 3], 16)

            @blk.tensor
            def _(te):
                for grp in range(2):
                    if grp == 1:
                        te.wait_ge(sm["vY"].h, bVY + 8)
                    for jc in range(44):
                        gj = grp * 44 + jc
                        te.wait_ge(dRL2[gj % 3], bRL2[gj % 3] + 16 * (gj // 3 + 1))
                        for k8 in range(8):
                            dc = grp * 8 + k8
                            ins = te.matmul(yps[k8][:], dslab[gj % 3][:, 128 * k8:128 * (k8 + 1)],
                                            h_t[:, jc, :], start=(jc == 0), stop=(jc == 43))
                        ins.then_inc(sm["tF"].h, 1)

            @blk.vector
            def _(v):
                v = Chain(v, "vCH")
                for grp in range(2):
                    v.wait_ge(sm["tF"].h, bTF + 44 * (grp + 1))
                    for k8 in range(8):
                        v.tensor_copy(yf[:, grp * 8 + k8, :], yps[k8][:], _lag=None)
                        v.mark("vY")
                v.finish()
        sm["dRL20"].v += 16 * 30
        sm["dRL21"].v += 16 * 29
        sm["dRL22"].v += 16 * 29
        sm["tF"].v += 88

        bTF2, bVY2, bAS2 = sm["tF"].v, sm["vY"].v, sm["aS"].v
        with (
            nc.psum_tensor([1, 512], F32) as sq1_ps,
            nc.psum_tensor([1, 512], F32) as sum2_ps,
            nc.psum_tensor([1, 512], F32) as sq2_ps,
            nc.psum_tensor([128, 512], F32) as b0_ps,
            nc.psum_tensor([128, 512], F32) as b1_ps,
            nc.Block() as blk,
        ):
            @blk.scalar
            def _(sc):
                sc = Chain(sc, "aCH")
                for dc in range(16):
                    sc.wait_ge(sm["vY"].h, bVY2 - 16 + dc + 1)
                    if dc >= 2:
                        sc.wait_ge(sm["tF"].h, bTF2 + dc - 1)
                    sc.activation(sqs[dc % 2][:], yf[:, dc, :], AF.Square,
                                  _lag=None)
                    sc.mark("aS")
                sc.wait_ge(sm["vY"].h, bVY2 + 1)
                sc.activation(r2[0:1, :], r1[0:1, :], AF.Ln, bias=ceps2[0:1, 0:1],
                              _lag=None)
                sc.mark("aS")
                sc.activation(r1[0:1, :], r2[0:1, :], AF.Exp, scale=-0.5, _lag=1)
                sc.mark("aS")
                for dc in range(16):
                    sc.wait_ge(sm["vY"].h, bVY2 + 1 + dc + 1)
                    if dc >= 2:
                        sc.wait_ge(sm["tF"].h, bTF2 + 16 + dc)
                    sc.activation(sqs[dc % 2][:], yf[:, dc, :], AF.Square,
                                  _lag=None)
                    sc.mark("aS")
                sc.wait_ge(sm["vY"].h, bVY2 + 1 + 16 + 2)
                sc.activation(r2[0:1, :], r1[0:1, :], AF.Ln, bias=ceps1[0:1, 0:1],
                              _lag=None)
                sc.mark("aS")
                sc.activation(r1[0:1, :], r2[0:1, :], AF.Exp, scale=-0.5, _lag=1)
                sc.mark("aS")
                sc.finish()

            @blk.tensor
            def _(te):
                te.wait_ge(sm["vY"].h, bVY2)
                for dc in range(16):
                    te.wait_ge(sm["aS"].h, bAS2 + dc + 1)
                    te.matmul(sq1_ps[:], ones_f[:], sqs[dc % 2][:],
                              start=(dc == 0), stop=(dc == 15)).then_inc(sm["tF"].h, 1)
                te.wait_ge(sm["aS"].h, bAS2 + 18)
                te.matmul(b0_ps[:], ones_k1[:], r1[0:1, :], start=True, stop=True
                          ).then_inc(sm["tF"].h, 1)
                for dc in range(16):
                    te.wait_ge(sm["vY"].h, bVY2 + 1 + dc + 1)
                    te.matmul(sum2_ps[:], ones_f[:], yf[:, dc, :],
                              start=(dc == 0), stop=(dc == 15))
                    te.wait_ge(sm["aS"].h, bAS2 + 18 + dc + 1)
                    te.matmul(sq2_ps[:], ones_f[:], sqs[dc % 2][:],
                              start=(dc == 0), stop=(dc == 15)).then_inc(sm["tF"].h, 1)
                te.wait_ge(sm["vY"].h, bVY2 + 1 + 16 + 2)
                te.matmul(b0_ps[:], ones_k1[:], r0[0:1, :], start=True, stop=True)
                te.wait_ge(sm["aS"].h, bAS2 + 36)
                te.matmul(b1_ps[:], ones_k1[:], r1[0:1, :], start=True, stop=True
                          ).then_inc(sm["tF"].h, 1)

            @blk.vector
            def _(v):
                v = Chain(v, "vCH")
                v.wait_ge(sm["tF"].h, bTF2 + 16)
                v.tensor_scalar_mul(r1[0:1, :], sq1_ps[0:1, :], INV_D, _lag=None)
                v.mark("vY")
                v.wait_ge(sm["tF"].h, bTF2 + 17)
                v.tensor_copy(bc0[:], b0_ps[:], _lag=None)
                for dc in range(0, 16, 2):
                    v.tensor_mul(xof32[:, dc, :], yf[:, dc, :], bc0[:],
                                 _lag=1 if dc == 0 else 2)
                    v.tensor_mul(xof32[:, dc + 1, :], yf[:, dc + 1, :], bc0[:],
                                 _lag=2)
                    v.tensor_add(yf[:, dc, :], xof32[:, dc, :], x2b[:, dc, :],
                                 _lag=2)
                    v.mark("vY")
                    v.tensor_add(yf[:, dc + 1, :], xof32[:, dc + 1, :],
                                 x2b[:, dc + 1, :], _lag=2)
                    v.mark("vY")
                v.wait_ge(sm["tF"].h, bTF2 + 33)
                v.tensor_scalar_mul(r0[0:1, :], sum2_ps[0:1, :], INV_D, _lag=None)
                v.tensor_scalar_mul(r1[0:1, :], sq2_ps[0:1, :], INV_D, _lag=None)
                v.mark("vY")
                v.tensor_mul(r2[0:1, :], r0[0:1, :], r0[0:1, :], _lag=2)
                v.tensor_sub(r1[0:1, :], r1[0:1, :], r2[0:1, :], _lag=1)
                v.mark("vY")
                v.wait_ge(sm["tF"].h, bTF2 + 34)
                v.tensor_copy(bc0[:], b0_ps[:], _lag=None)
                v.tensor_copy(bc1[:], b1_ps[:], _lag=None)
                for dc in range(0, 16, 2):
                    v.tensor_sub(xof32[:, dc, :], yf[:, dc, :], bc0[:], _lag=2)
                    v.tensor_sub(xof32[:, dc + 1, :], yf[:, dc + 1, :], bc0[:],
                                 _lag=2)
                    v.tensor_mul(yf[:, dc, :], xof32[:, dc, :], bc1[:], _lag=2)
                    v.mark("vY")
                    v.tensor_mul(yf[:, dc + 1, :], xof32[:, dc + 1, :], bc1[:],
                                 _lag=2)
                    v.mark("vY")
                v.finish()

            @blk.sync
            def _(sy):
                sy.wait_ge(sm["vY"].h, bVY2 + 35)
                sy.dma_start(out=out_t[:, :, SPW * s:SPW * (s + 1)], in_=yf[:]
                             ).then_inc(sm["dOUT"].h, 16)
        sm["tF"].v += 34
        sm["dOUT"].v += 16
        nc.all_engine_barrier()

    with nc.Block() as blk:
        @blk.sync
        def _(sy):
            sy.wait_ge(sm["dOUT"].h, sm["dOUT"].v)
    return nc


def _rope_tables():
    inv_freq = 1.0 / (10000.0 ** (np.arange(0, DH, 2, dtype=np.float32) / DH))
    freqs = np.arange(S, dtype=np.float32)[:, None] * inv_freq[None, :]
    cos = np.repeat(np.cos(freqs), 2, axis=-1).astype(np.float32).T.copy()
    sin = np.repeat(np.sin(freqs), 2, axis=-1).astype(np.float32).T.copy()
    sin_eff = sin.copy()
    sin_eff[:64] *= -1.0
    return cos, sin_eff


def _mask_tiles(q0s):
    out = np.zeros((NSLOT, 128, 8, SPW), dtype=BF)
    kk = np.arange(128)[:, None]
    qq = np.arange(SPW)[None, :]
    for s_ in range(NSLOT):
        q0 = q0s[s_]
        FF = SLOT_F[s_]
        for m in range(8):
            kt = FF + m
            if kt >= SLOT_COUNTS[s_]:
                continue
            k0 = kt * 128
            off = (k0 - q0) // 128
            if off <= 0:
                out[s_, :, m, :] = 1.0
            else:
                out[s_, :, m, :] = ((k0 + kk) <= (q0 + qq + WIN)).astype(np.float32)
    return out


def kernel(L0_attn_x, L0_attn_Wq, L0_attn_Wk, L0_attn_Wv, L0_attn_Wo,
           L0_ffn_gate_up_packed, L0_ffn_gate_up_scales,
           L0_ffn_down_packed, L0_ffn_down_scales):
    x = np.ascontiguousarray(np.asarray(L0_attn_x, dtype=np.float32))
    wq = np.asarray(L0_attn_Wq, dtype=np.float32)
    wk = np.asarray(L0_attn_Wk, dtype=np.float32)
    wv = np.asarray(L0_attn_Wv, dtype=np.float32)
    wo = np.asarray(L0_attn_Wo, dtype=np.float32)
    gup = np.asarray(L0_ffn_gate_up_packed, dtype=np.int32)
    gus = np.asarray(L0_ffn_gate_up_scales)
    dnp = np.asarray(L0_ffn_down_packed, dtype=np.int32)
    dns = np.asarray(L0_ffn_down_scales)

    cosk, sink = _rope_tables()

    wqT = wq.T[pi_d]
    wq_pack = np.ascontiguousarray(
        wqT.reshape(16, 128, NH, 128).transpose(2, 1, 0, 3))
    wkT = wk.T[pi_d]
    wk_pack = np.ascontiguousarray(wkT.reshape(16, 128, 512).transpose(1, 0, 2))
    wvT = wv.T[pi_d]
    wv_pack = np.ascontiguousarray(wvT.reshape(16, 128, 512).transpose(1, 0, 2))
    woT = wo.T[:, pi_d]
    wo_pack = np.ascontiguousarray(
        woT.reshape(NH, 128, 16, 128).transpose(2, 1, 0, 3))
    gate_p, up_p = gup[:II][pi_j], gup[II:][pi_j]
    gusT = np.asarray(gus, np.float16).T
    gate_s, up_s = gusT[:II][pi_j], gusT[II:][pi_j]
    gup_pack = np.empty((88, 128, 256), np.int32)
    gus_pack = np.empty((88, 128, 16), np.float16)
    for j in range(44):
        gup_pack[2 * j] = gate_p[128 * j:128 * (j + 1)]
        gup_pack[2 * j + 1] = up_p[128 * j:128 * (j + 1)]
        gus_pack[2 * j] = gate_s[128 * j:128 * (j + 1)]
        gus_pack[2 * j + 1] = up_s[128 * j:128 * (j + 1)]
    dnp_pack = np.ascontiguousarray(dnp[pi_d].reshape(16, 128, 704))
    dns_pack = np.ascontiguousarray(
        np.asarray(dns, np.float16).T[pi_d].reshape(16, 128, 44))
    pswap_np = np.zeros((128, 128), dtype=BF)
    for a in range(128):
        pswap_np[(a + 64) % 128, a] = 1.0
    ident_np = np.eye(128, dtype=np.float32).astype(BF)
    cosk_np = np.ascontiguousarray(cosk)
    sink_np = np.ascontiguousarray(sink)


    nc = build_program()

    in_maps = []
    span_sets = [SPANS_A, SPANS_B]
    out_meta = []
    for core in range(8):
        b, half = divmod(core, 2)
        spans = span_sets[half]
        own_cols = np.concatenate([np.arange(sp * SPW, (sp + 1) * SPW) for sp in spans])
        q0s = [sp * SPW for sp in spans]
        xT = x[b].T[pi_d]
        x_pack = np.ascontiguousarray(
            xT.reshape(16, 128, 8, SPW).transpose(2, 1, 0, 3))
        xo = xT[:, own_cols]
        xo_pack = np.ascontiguousarray(
            xo.reshape(16, 128, NSLOT, SPW).transpose(2, 1, 0, 3))
        cosq_np = np.ascontiguousarray(
            cosk[:, own_cols].reshape(128, NSLOT, SPW).transpose(1, 0, 2))
        sinq_np = np.ascontiguousarray(
            sink[:, own_cols].reshape(128, NSLOT, SPW).transpose(1, 0, 2))
        masks_np = _mask_tiles(q0s)
        in_maps.append({
            "x_in": x_pack, "xo_in": xo_pack,
            "wq_in": wq_pack, "wo_in": wo_pack, "wk_in": wk_pack, "wv_in": wv_pack,
            "gup_in": gup_pack, "gus_in": gus_pack,
            "dnp_in": dnp_pack, "dns_in": dns_pack,
            "cosk_in": cosk_np, "sink_in": sink_np,
            "cosq_in": cosq_np, "sinq_in": sinq_np,
            "pswap_in": pswap_np, "masks_in": masks_np, "ident_in": ident_np,
        })
        out_meta.append((b, own_cols))

    res = run_bass_kernel_spmd(nc, in_maps, core_ids=list(range(8)))

    out = np.empty((4, S, D), np.float32)
    for core in range(8):
        b, own_cols = out_meta[core]
        ot = np.asarray(res.results[core]["out_t"])
        operm = ot.transpose(1, 0, 2).reshape(D, 4 * SPW)
        ounp = np.empty_like(operm)
        ounp[pi_d] = operm
        out[b, own_cols, :] = ounp.T
    return out


# version 17
# speedup vs baseline: 1.0904x; 1.0052x over previous
import numpy as np
import ml_dtypes

import concourse.bass as bass
import concourse.mybir as mybir
from concourse.bass_utils import run_bass_kernel_spmd

F32 = mybir.dt.float32
BF16 = mybir.dt.bfloat16
F16 = mybir.dt.float16
I32 = mybir.dt.int32
AF = mybir.ActivationFunctionType
ALU = mybir.AluOpType
BF = ml_dtypes.bfloat16

S = 4096
D = 2048
NH, KVH, DH = 16, 4, 128
II = 5632
WIN = 128
SPW = 512
NSLOT = 4
SLOT_COUNTS = [9, 17, 25, 32]
SLOT_F = [1, 9, 17, 25]
SPANS_A = [0, 2, 5, 7]
SPANS_B = [1, 3, 4, 6]
SCALE = float(DH ** -0.5)
EXP_BIAS = -4.0
P126 = float(2.0 ** 126)
INV_D = 1.0 / D

pi_d = np.array([4 * (p % 512) + p // 512 for p in range(D)])
pi_j = np.array([4 * (p % 1408) + p // 1408 for p in range(II)])


def build_program():
    nc = bass.Bass("TRN2", target_bir_lowering=False)

    x_in = nc.dram_tensor("x_in", [8, 128, 16, SPW], F32, kind="ExternalInput").ap()
    xo_in = nc.dram_tensor("xo_in", [NSLOT, 128, 16, SPW], F32, kind="ExternalInput").ap()
    wq_in = nc.dram_tensor("wq_in", [NH, 128, 16, 128], F32, kind="ExternalInput").ap()
    wo_in = nc.dram_tensor("wo_in", [16, 128, NH, 128], F32, kind="ExternalInput").ap()
    wk_in = nc.dram_tensor("wk_in", [128, 16, 512], F32, kind="ExternalInput").ap()
    wv_in = nc.dram_tensor("wv_in", [128, 16, 512], F32, kind="ExternalInput").ap()
    gup_in = nc.dram_tensor("gup_in", [88, 128, 256], I32, kind="ExternalInput").ap()
    gus_in = nc.dram_tensor("gus_in", [88, 128, 16], F16, kind="ExternalInput").ap()
    dnp_in = nc.dram_tensor("dnp_in", [16, 128, 704], I32, kind="ExternalInput").ap()
    dns_in = nc.dram_tensor("dns_in", [16, 128, 44], F16, kind="ExternalInput").ap()
    cosk_in = nc.dram_tensor("cosk_in", [128, S], F32, kind="ExternalInput").ap()
    sink_in = nc.dram_tensor("sink_in", [128, S], F32, kind="ExternalInput").ap()
    cosq_in = nc.dram_tensor("cosq_in", [NSLOT, 128, SPW], F32, kind="ExternalInput").ap()
    sinq_in = nc.dram_tensor("sinq_in", [NSLOT, 128, SPW], F32, kind="ExternalInput").ap()
    pswap_in = nc.dram_tensor("pswap_in", [128, 128], BF16, kind="ExternalInput").ap()
    masks_in = nc.dram_tensor("masks_in", [NSLOT, 128, 8, SPW], BF16, kind="ExternalInput").ap()
    w3g = nc.dram_tensor("w3g", [22, 128, 16, 512], BF16, kind="Internal").ap()
    w3d = nc.dram_tensor("w3d", [2, 44, 128, 1024], BF16, kind="Internal").ap()
    ident_in = nc.dram_tensor("ident_in", [128, 128], BF16, kind="ExternalInput").ap()
    out_t = nc.dram_tensor("out_t", [128, 16, 4 * SPW], F32, kind="ExternalOutput").ap()

    class Sem:
        def __init__(self, name):
            self.h = nc.alloc_semaphore(name)
            self.v = 0

    sm = {n: Sem(n) for n in
          ["dPK0", "dPK1", "dX0", "dX1", "dXO", "dXF", "dW", "dWq0", "dWq1",
           "dM", "dKT", "dSP0", "dSP1", "dRL0", "dRL1", "dRL20", "dRL21",
           "dGS0", "dGS1", "dRLg2", "dRL22", "dWq2",
           "dOUT",
           "vD", "vK", "vQ", "vA", "vE", "vO", "vF", "vY", "aE", "aS", "aD",
           "tS", "tAV", "tKV", "tQ", "tWo", "tF", "vCH", "aCH"]}

    _CHAINED = {"tensor_tensor", "tensor_scalar", "tensor_single_scalar",
                "scalar_tensor_tensor", "tensor_copy", "tensor_add",
                "tensor_sub", "tensor_mul", "tensor_max", "tensor_scalar_mul",
                "tensor_scalar_add", "memset", "reciprocal", "activation",
                "mul", "add", "copy", "square", "sqrt"}

    import collections

    class Chain:

        _hist = {}

        def __init__(self, eng, sem_name):
            self._e = eng
            self._s = sm[sem_name]
            self._name = sem_name
            self._pending = None
            self._h = Chain._hist.setdefault(sem_name,
                                             collections.deque(maxlen=16))

        def _flush(self):
            if self._pending is not None:
                self._pending.then_inc(self._s.h, 1)
                self._s.v += 1
                self._h.append((self._s.h, self._s.v))
                self._pending = None

        def mark(self, sem_name, inc=1):
            assert self._pending is not None
            ms = sm[sem_name]
            self._pending.then_inc(ms.h, inc)
            ms.v += inc
            self._h.append((ms.h, ms.v))
            self._pending = None

        def finish(self):
            self._flush()
            if self._h:
                self._e.wait_ge(*self._h[-1])

        def __getattr__(self, name):
            f = getattr(self._e, name)
            if name not in _CHAINED:
                return f

            def wrap(*a, _lag=1, **kw):
                self._flush()
                if _lag is not None and len(self._h) >= _lag:
                    self._e.wait_ge(*self._h[-_lag])
                ins = f(*a, **kw)
                self._pending = ins
                return ins
            return wrap

    K_t = nc.alloc_sbuf_tensor("K_t", [128, KVH, S], BF16)
    V_t = nc.alloc_sbuf_tensor("V_t", [128, 32, 512], BF16)
    pswap = nc.alloc_sbuf_tensor("pswap", [128, 128], BF16)
    ones_b = nc.alloc_sbuf_tensor("ones_b", [128, 1], BF16)
    ones_f = nc.alloc_sbuf_tensor("ones_f", [128, 1], F32)
    ones_k1 = nc.alloc_sbuf_tensor("ones_k1", [1, 128], F32)
    s6 = nc.alloc_sbuf_tensor("s6", [128, 1], I32)
    cb4 = nc.alloc_sbuf_tensor("cb4", [128, 1], F32)
    ceps1 = nc.alloc_sbuf_tensor("ceps1", [128, 1], F32)
    ceps2 = nc.alloc_sbuf_tensor("ceps2", [128, 1], F32)
    xob16 = nc.alloc_sbuf_tensor("xob16", [128, 16, SPW], BF16)
    ident = nc.alloc_sbuf_tensor("ident", [128, 128], BF16)

    DTSZ = {F32: 4, BF16: 2, F16: 2, I32: 4}
    SBUF_TOP = 229344
    overlay_specs = [
        ("pk0", [128, 704], I32, {"W"}), ("pk1", [128, 704], I32, {"W"}),
        ("sc0", [128, 44], F16, {"W"}), ("sc1", [128, 44], F16, {"W"}),
        ("scf0", [128, 1408], BF16, {"W"}), ("scf1", [128, 1408], BF16, {"W"}),
        ("u7t", [128, 4, 704], I32, {"W"}), ("sgt", [128, 4, 704], I32, {"W"}),
        ("bitst", [128, 4, 704], I32, {"W"}),
        ("dec0", [128, 4, 1408], BF16, {"W"}), ("dec1", [128, 4, 1408], BF16, {"W"}),
        ("tsp0", [128, 44, 128], BF16, {"W"}), ("tsp1", [128, 44, 128], BF16, {"W"}),
        ("xb0", [128, 16, 512], BF16, {"P0"}), ("xb1", [128, 16, 512], BF16, {"P0"}),
        ("wk_t", [128, 16, 512], BF16, {"P0"}), ("wv_t", [128, 16, 512], BF16, {"P0"}),
        ("cosk_t", [128, S], F32, {"P0"}), ("sink_t", [128, S], F32, {"P0"}),
        ("masks_t", [128, 8, SPW], BF16, {"Q", "A"}),
        ("cosq_t", [128, SPW], F32, {"Q"}), ("sinq_t", [128, SPW], F32, {"Q"}),
        ("wblk0", [128, 16, 128], BF16, {"Q", "O"}),
        ("wblk1", [128, 16, 128], BF16, {"Q", "O"}),
        ("wblk2", [128, 16, 128], BF16, {"Q", "O"}),
        ("Q_t", [128, NH, SPW], BF16, {"Q", "A"}),

        ("On_t", [128, NH, SPW], BF16, {"A", "O"}),
        ("xof32", [128, 16, SPW], F32, {"O", "L2"}),
        ("x2b", [128, 16, SPW], BF16, {"O", "Fg", "Fd", "L2"}),
        ("h_t", [128, 44, SPW], BF16, {"Fg", "Fd"}),
        ("gslab0", [128, 16, 512], BF16, {"O", "Fg"}),
        ("gslab1", [128, 16, 512], BF16, {"Fg"}),
        ("gslab2", [128, 16, 512], BF16, {"Fg"}),
        ("dslab0", [128, 1024], BF16, {"Fd"}), ("dslab1", [128, 1024], BF16, {"Fd"}),
        ("dslab2", [128, 1024], BF16, {"Fd"}),
        ("yf", [128, 16, SPW], F32, {"Fd", "L2"}),
        ("lrow", [1, SPW], F32, {"A"}), ("lrec", [1, SPW], F32, {"A"}),
        ("lln", [1, SPW], F32, {"A"}),
        ("r0", [1, SPW], F32, {"O", "L2"}), ("r1", [1, SPW], F32, {"O", "L2"}),
        ("r2", [1, SPW], F32, {"O", "L2"}),
        ("bc0", [128, SPW], F32, {"A", "O", "L2"}),
        ("bc1", [128, SPW], F32, {"O", "L2"}),
        ("P_t0", [128, 2 * SPW], BF16, {"A"}), ("P_t1", [128, 2 * SPW], BF16, {"A"}),
        ("qtmp", [128, SPW], BF16, {"P0", "Q"}),
        ("t1f", [128, SPW], F32, {"P0", "Q"}), ("t2f", [128, SPW], F32, {"P0", "Q"}),
        ("sqs0", [128, SPW], F32, {"O", "L2"}), ("sqs1", [128, SPW], F32, {"O", "L2"}),
        ("silu0", [128, SPW], F32, {"Fg"}), ("silu1", [128, SPW], F32, {"Fg"}),
    ]
    def _pack(specs):
        used, placed = [], {}
        for name, shape, dt_, phases in specs:
            nbytes = int(np.prod(shape[1:])) * DTSZ[dt_]
            nbytes = (nbytes + 31) // 32 * 32
            cand = 0
            while True:
                conflict = None
                for (s0_, e0_, ph0) in used:
                    if cand < e0_ and cand + nbytes > s0_ and (phases & ph0):
                        conflict = e0_
                        break
                if conflict is None:
                    break
                cand = conflict
            used.append((cand, cand + nbytes, set(phases)))
            placed[name] = (cand, shape, dt_)
        return max(e for _, e, _ in used), placed

    base = sorted(overlay_specs, key=lambda t: -int(np.prod(t[1][1:])) * DTSZ[t[2]])
    ov_size, placed = _pack(base)
    rng = np.random.RandomState(7)
    for _ in range(200):
        cand_specs = list(base)
        rng.shuffle(cand_specs)
        sz, pl = _pack(cand_specs)
        if sz < ov_size:
            ov_size, placed = sz, pl
    ov_base = SBUF_TOP - ov_size
    rem = nc.sbuf_bytes_remaining
    assert ov_size + 2048 <= rem, (ov_size, rem)
    ov = {}
    for name, (off, shape, dt_) in placed.items():
        ov[name] = nc.alloc_sbuf_tensor_at(name, shape, dt_, offset=ov_base + off)

    masks_t, cosq_t, sinq_t = ov["masks_t"], ov["cosq_t"], ov["sinq_t"]
    Q_t, On_t = ov["Q_t"], ov["On_t"]
    xof32, x2b, yf, h_t = ov["xof32"], ov["x2b"], ov["yf"], ov["h_t"]
    wblk = [ov["wblk0"], ov["wblk1"], ov["wblk2"]]
    gslab = [ov["gslab0"], ov["gslab1"], ov["gslab2"]]
    dslab = [ov["dslab0"], ov["dslab1"], ov["dslab2"]]
    lrow, lrec, lln = ov["lrow"], ov["lrec"], ov["lln"]
    r0, r1, r2 = ov["r0"], ov["r1"], ov["r2"]
    bc0, bc1 = ov["bc0"], ov["bc1"]
    P_t = [ov["P_t0"], ov["P_t1"]]
    qtmp, t1f, t2f = ov["qtmp"], ov["t1f"], ov["t2f"]
    sqs = [ov["sqs0"], ov["sqs1"]]
    silu_sb = [ov["silu0"], ov["silu1"]]

    with nc.Block() as blk:
        @blk.vector
        def _(v):
            v = Chain(v, "vCH")
            v.memset(s6[:], 6)
            v.memset(cb4[:], EXP_BIAS)
            v.memset(ceps1[:], 1e-5)
            v.memset(ceps2[:], 1e-6)
            v.memset(ones_b[:], 1.0)
            v.memset(ones_f[:], 1.0)
            v.memset(ones_k1[:], 1.0)
            v.finish()

        @blk.sync
        def _(sy):
            sy.dma_start(out=pswap[:], in_=pswap_in[:, :]).then_inc(sm["dM"].h, 16)
            sy.dma_start(out=ident[:], in_=ident_in[:, :]).then_inc(sm["dM"].h, 16)

        @blk.gpsimd
        def _(gp):
            gp.dma_start(out=xob16[:], in_=xo_in[0]).then_inc(sm["dXO"].h, 16)
    sm["dM"].v += 32
    sm["dXO"].v += 16
    nc.all_engine_barrier()

    chunks = [("g", c, 256, 16) for c in range(88)] + [("d", c, 704, 44) for c in range(16)]
    NCH = len(chunks)
    grp_of_chunk = []
    gtot = 0
    for (kind, c, W, G) in chunks:
        nj = (2 * W * 4) // 128 // 4
        grp_of_chunk.append((gtot, nj))
        gtot += nj
    bPK = [sm["dPK0"].v, sm["dPK1"].v]
    bSP = [sm["dSP0"].v, sm["dSP1"].v]
    bVD, bTT, bVT = sm["vD"].v, sm["tKV"].v, sm["vK"].v
    bAD = sm["aD"].v
    dPK = [sm["dPK0"].h, sm["dPK1"].h]
    dSP = [sm["dSP0"].h, sm["dSP1"].h]
    scfb = [ov["scf0"], ov["scf1"]]
    u7t, sgt, bitst = ov["u7t"], ov["sgt"], ov["bitst"]
    with (
        nc.psum_tensor([128, 1024], BF16) as tp0,
        nc.psum_tensor([128, 1024], BF16) as tp1,
        nc.Block() as blk,
    ):
        tps_ = [tp0, tp1]
        pk = [ov["pk0"], ov["pk1"]]
        scb = [ov["sc0"], ov["sc1"]]
        dec = [ov["dec0"], ov["dec1"]]
        tsp = [ov["tsp0"], ov["tsp1"]]

        @blk.sync
        def _(sy):
            for ci, (kind, c, W, G) in enumerate(chunks):
                if ci >= 2:
                    sy.wait_ge(sm["vD"].h, bVD + ci - 1)
                    sy.wait_ge(sm["aD"].h, bAD + ci - 1)
                src_p = gup_in[c] if kind == "g" else dnp_in[c]
                src_s = gus_in[c] if kind == "g" else dns_in[c]
                sy.dma_start(out=pk[ci % 2][:, :W], in_=src_p).then_inc(dPK[ci % 2], 16)
                sy.dma_start(out=scb[ci % 2][:, :G], in_=src_s).then_inc(dPK[ci % 2], 16)

        @blk.vector
        def _(v):
            v = Chain(v, "vCH")
            for ci, (kind, c, W, G) in enumerate(chunks):
                g0, ng = grp_of_chunk[ci]
                v.wait_ge(dPK[ci % 2], bPK[ci % 2] + 32 * (ci // 2 + 1))
                if ci >= 2:
                    v.wait_ge(sm["tKV"].h, bTT + grp_of_chunk[ci - 2][0]
                              + grp_of_chunk[ci - 2][1])
                x = pk[ci % 2][:, :W]
                for k in range(4):
                    sh = 6 - 4 * k
                    v.tensor_scalar(u7t[:, k, :W], x, abs(sh), 0x01C001C0,
                                    ALU.logical_shift_left if sh >= 0
                                    else ALU.logical_shift_right,
                                    ALU.bitwise_and, _lag=4)
                for k in range(4):
                    v.tensor_scalar(sgt[:, k, :W], x, 12 - 4 * k, -2147450880,
                                    ALU.logical_shift_left, ALU.bitwise_and,
                                    _lag=4)
                for k in range(4):
                    v.tensor_tensor(bitst[:, k, :W], u7t[:, k, :W],
                                    sgt[:, k, :W], ALU.bitwise_or, _lag=4)
                v.wait_ge(sm["aD"].h, bAD + ci + 1)
                for k in range(4):
                    ins = v.tensor_tensor(
                        dec[ci % 2][:, k, :2 * W],
                        bitst[:, k, :W].bitcast(BF16),
                        scfb[ci % 2][:, :2 * W],
                        ALU.mult, _lag=4)
                v.mark("vD")
            v.finish()

        @blk.scalar
        def _(sc):
            sc = Chain(sc, "aCH")

            def emit_scf(cj):
                _, _, Wj, Gj = chunks[cj]
                sc.wait_ge(dPK[cj % 2], bPK[cj % 2] + 32 * (cj // 2 + 1))
                if cj >= 2:
                    sc.wait_ge(sm["vD"].h, bVD + cj - 1)
                scbx = scb[cj % 2]
                sc.activation(
                    scfb[cj % 2][:, :2 * Wj].rearrange("p (g j) -> p g j", j=32),
                    bass.AP(scbx.ap().tensor, scbx.ap().offset,
                            [scbx.ap().ap[0], [1, Gj], [0, 32]]),
                    AF.Copy, scale=P126, _lag=None)
                sc.mark("aD")

            emit_scf(0)
            for ci, (kind, c, W, G) in enumerate(chunks):
                g0, ng = grp_of_chunk[ci]
                if ci + 1 < NCH:
                    emit_scf(ci + 1)
                for g in range(ng):
                    if ci >= 2 and g == 0:
                        sc.wait_ge(dSP[ci % 2], bSP[ci % 2] + 16 * (ci // 2))
                    sc.wait_ge(sm["tKV"].h, bTT + g0 + g + 1)
                    sc.activation(tsp[ci % 2][:, 4 * g:4 * g + 4, :],
                                  tps_[(g0 + g) % 2][:, :512], AF.Copy,
                                  _lag=None)
                    sc.mark("vK")
            sc.finish()

        @blk.tensor
        def _(te):
            te.wait_ge(sm["dM"].h, 32)
            for ci, (kind, c, W, G) in enumerate(chunks):
                g0, ng = grp_of_chunk[ci]
                te.wait_ge(sm["vD"].h, bVD + ci + 1)
                ppk = (2 * W) // 128
                for g in range(ng):
                    if g0 + g >= 2:
                        te.wait_ge(sm["vK"].h, bVT + g0 + g - 1)
                    for jj in range(4):
                        j = 4 * g + jj
                        piece = dec[ci % 2][:, j // ppk,
                                            128 * (j % ppk):128 * (j % ppk) + 128]
                        ins = te.transpose(
                            tps_[(g0 + g) % 2][:, 128 * jj:128 * (jj + 1)],
                            piece, ident[:])
                    ins.then_inc(sm["tKV"].h, 1)

        @blk.gpsimd
        def _(gp):
            for ci, (kind, c, W, G) in enumerate(chunks):
                g0, ng = grp_of_chunk[ci]
                gp.wait_ge(sm["vK"].h, bVT + g0 + ng)
                if kind == "g":
                    dst = w3g[c // 4, :, :, 128 * (c % 4):128 * (c % 4) + 128]
                    src = tsp[ci % 2][:, :16, :]
                else:
                    dst = w3d[c // 8, :, :, 128 * (c % 8):128 * (c % 8) + 128
                              ].rearrange("a p o -> p a o")
                    src = tsp[ci % 2][:, :44, :]
                gp.dma_start(out=dst, in_=src).then_inc(dSP[ci % 2], 16)
            gp.wait_ge(dSP[0], bSP[0] + 16 * ((NCH + 1) // 2))
            gp.wait_ge(dSP[1], bSP[1] + 16 * (NCH // 2))
    sm["dPK0"].v += 32 * ((NCH + 1) // 2)
    sm["dPK1"].v += 32 * (NCH // 2)
    sm["dSP0"].v += 16 * ((NCH + 1) // 2)
    sm["dSP1"].v += 16 * (NCH // 2)
    sm["tKV"].v += gtot
    nc.all_engine_barrier()

    bW, bKV, bVK = sm["dW"].v, sm["tKV"].v, sm["vK"].v
    wk_t, wv_t = ov["wk_t"], ov["wv_t"]
    cosk_t, sink_t = ov["cosk_t"], ov["sink_t"]
    with (
        nc.psum_tensor([128, 512], F32) as kps0,
        nc.psum_tensor([128, 512], F32) as kps1,
        nc.psum_tensor([128, 512], F32) as sps0,
        nc.psum_tensor([128, 512], F32) as sps1,
        nc.psum_tensor([128, 512], F32) as vps0,
        nc.psum_tensor([128, 512], F32) as vps1,
        nc.Block() as blk,
    ):
        xb = [ov["xb0"], ov["xb1"]]
        kps, sps, vps = [kps0, kps1], [sps0, sps1], [vps0, vps1]

        dX = [sm["dX0"].h, sm["dX1"].h]
        bXp = [sm["dX0"].v, sm["dX1"].v]

        @blk.sync
        def _(sy):
            sy.dma_start(out=cosk_t[:], in_=cosk_in[:, :]).then_inc(sm["dKT"].h, 16)
            sy.dma_start(out=sink_t[:], in_=sink_in[:, :]).then_inc(sm["dKT"].h, 16)

        @blk.gpsimd
        def _(gp):
            gp.dma_start(out=wk_t[:], in_=wk_in[:, :, :]).then_inc(sm["dW"].h, 16)
            gp.dma_start(out=wv_t[:], in_=wv_in[:, :, :]).then_inc(sm["dW"].h, 16)
            for sp in range(8):
                if sp >= 2:
                    gp.wait_ge(sm["tKV"].h, bKV + 12 * (sp - 1))
                gp.dma_start(out=xb[sp % 2][:], in_=x_in[sp]).then_inc(dX[sp % 2], 16)

        @blk.tensor
        def _(te):
            te.wait_ge(sm["dW"].h, bW + 32)
            te.wait_ge(sm["dM"].h, 32)
            for sp in range(8):
                te.wait_ge(dX[sp % 2], bXp[sp % 2] + 16 * (sp // 2 + 1))
                for kvh in range(KVH):
                    pos = sp * 4 + kvh
                    if pos >= 2:
                        spp, kvp = divmod(pos - 2, 4)
                        te.wait_ge(sm["vK"].h, bVK + spp * 12 + 2 * kvp + 2)
                    for c in range(16):
                        ins = te.matmul(kps[kvh % 2][:], wk_t[:, c, 128 * kvh:128 * (kvh + 1)],
                                        xb[sp % 2][:, c, :], start=(c == 0), stop=(c == 15))
                    ins.then_inc(sm["tKV"].h, 1)
                    te.wait_ge(sm["vK"].h, bVK + sp * 12 + 2 * kvh + 1)
                    te.matmul(sps[kvh % 2][:], pswap[:], qtmp[:],
                              start=True, stop=True).then_inc(sm["tKV"].h, 1)
                for tc in range(4):
                    pos = sp * 4 + tc
                    if pos >= 2:
                        spp, tcp = divmod(pos - 2, 4)
                        te.wait_ge(sm["vK"].h, bVK + spp * 12 + 8 + tcp + 1)
                    for c in range(16):
                        ins = te.matmul(vps[tc % 2][:], xb[sp % 2][:, c, 128 * tc:128 * (tc + 1)],
                                        wv_t[:, c, :], start=(c == 0), stop=(c == 15))
                    ins.then_inc(sm["tKV"].h, 1)

        @blk.vector
        def _(v):
            v = Chain(v, "vCH")
            v.wait_ge(sm["dKT"].h, 32)
            for sp in range(8):
                bt = bKV + sp * 12
                for kvh in range(KVH):
                    v.wait_ge(sm["tKV"].h, bt + 2 * kvh + 1)
                    v.tensor_copy(qtmp[:], kps[kvh % 2][:], _lag=None)
                    v.mark("vK")
                    v.wait_ge(sm["tKV"].h, bt + 2 * kvh + 2)
                    cs = cosk_t[:, 512 * sp:512 * (sp + 1)]
                    sn = sink_t[:, 512 * sp:512 * (sp + 1)]
                    v.tensor_mul(t1f[:], sps[kvh % 2][:], sn, _lag=2)
                    v.tensor_mul(t2f[:], kps[kvh % 2][:], cs, _lag=3)
                    v.tensor_add(K_t[:, kvh, 512 * sp:512 * (sp + 1)], t1f[:], t2f[:]
                                 )
                    v.mark("vK")
                for tc in range(4):
                    v.wait_ge(sm["tKV"].h, bt + 8 + tc + 1)
                    v.tensor_copy(V_t[:, 4 * sp + tc, :], vps[tc % 2][:], _lag=None)
                    v.mark("vK")
            v.finish()
    sm["dKT"].v += 32
    sm["dW"].v += 32
    sm["dX0"].v += 16 * 4
    sm["dX1"].v += 16 * 4
    sm["tKV"].v += 12 * 8
    nc.all_engine_barrier()

    for s in range(NSLOT):
        cnt, FF = SLOT_COUNTS[s], SLOT_F[s]

        bXO, bM = sm["dXO"].v, sm["dM"].v
        bOUT = sm["dOUT"].v
        bWq = [sm["dWq0"].v, sm["dWq1"].v, sm["dWq2"].v]
        dWq = [sm["dWq0"].h, sm["dWq1"].h, sm["dWq2"].h]
        bTQ, bVQ = sm["tQ"].v, sm["vQ"].v
        with (
            nc.psum_tensor([128, 512], F32) as qps0,
            nc.psum_tensor([128, 512], F32) as qps1,
            nc.psum_tensor([128, 512], F32) as qsp0,
            nc.psum_tensor([128, 512], F32) as qsp1,
            nc.Block() as blk,
        ):
            qps, qsp = [qps0, qps1], [qsp0, qsp1]

            @blk.sync
            def _(sy):
                sy.wait_ge(sm["dOUT"].h, bOUT)
                sy.dma_start(out=masks_t[:], in_=masks_in[s]).then_inc(sm["dM"].h, 16)
                sy.dma_start(out=cosq_t[:], in_=cosq_in[s]).then_inc(sm["dM"].h, 16)
                sy.dma_start(out=sinq_t[:], in_=sinq_in[s]).then_inc(sm["dM"].h, 16)

            @blk.gpsimd
            def _(gp):
                gp.wait_ge(sm["dOUT"].h, bOUT)
                for qh in range(NH):
                    if qh >= 3:
                        gp.wait_ge(sm["tQ"].h,
                                   bTQ + (1 if qh == 3 else 2 * (qh - 3)))
                    gp.dma_start(out=wblk[qh % 3][:], in_=wq_in[qh]
                                 ).then_inc(dWq[qh % 3], 16)

            @blk.vector
            def _(v):
                v = Chain(v, "vCH")
                v.wait_ge(sm["dOUT"].h, bOUT)
                v.wait_ge(sm["dM"].h, bM + 48)
                for qh in range(NH):
                    v.wait_ge(sm["tQ"].h, bTQ + (1 if qh == 0 else 2 * qh))
                    v.tensor_copy(qtmp[:], qps[qh % 2][:], _lag=None)
                    v.mark("vQ")
                    v.wait_ge(sm["tQ"].h,
                              bTQ + (2 * qh + 3 if qh <= NH - 2 else 2 * NH))
                    v.tensor_mul(t1f[:], qsp[qh % 2][:], sinq_t[:], _lag=2)
                    v.tensor_mul(t2f[:], qps[qh % 2][:], cosq_t[:], _lag=3)
                    v.tensor_add(Q_t[:, qh, :], t1f[:], t2f[:])
                    v.mark("vQ")
                v.finish()

            @blk.tensor
            def _(te):
                te.wait_ge(sm["dXO"].h, 16 * (s + 1))

                def emit_swap(qh):
                    te.wait_ge(sm["vQ"].h, bVQ + 2 * qh + 1)
                    te.matmul(qsp[qh % 2][:], pswap[:], qtmp[:],
                              start=True, stop=True).then_inc(sm["tQ"].h, 1)

                for qh in range(NH):
                    te.wait_ge(dWq[qh % 3], bWq[qh % 3] + 16 * (qh // 3 + 1))
                    if qh >= 2:
                        te.wait_ge(sm["vQ"].h, bVQ + 2 * (qh - 2) + 2)
                    for c in range(16):
                        ins = te.matmul(qps[qh % 2][:], wblk[qh % 3][:, c, :],
                                        xob16[:, c, :], start=(c == 0), stop=(c == 15))
                    ins.then_inc(sm["tQ"].h, 1)
                    if qh >= 1:
                        emit_swap(qh - 1)
                emit_swap(NH - 1)
        sm["dM"].v += 48
        sm["dWq0"].v += 16 * 6
        sm["dWq1"].v += 16 * 5
        sm["dWq2"].v += 16 * 5
        sm["tQ"].v += 2 * NH
        nc.all_engine_barrier()

        bS, bAV, bAE, bVA, bVE = (sm["tS"].v, sm["tAV"].v, sm["aE"].v,
                                  sm["vA"].v, sm["vE"].v)
        bAS_A = sm["aS"].v
        PP = (cnt + 1) // 2
        nmask_before = {}
        nm = 0
        for hi in range(NH):
            for kt in range(cnt):
                nmask_before[(hi, kt)] = nm
                if kt >= FF:
                    nm += 1

        def tav_after_pair(gp):
            hi2, p2 = divmod(gp, PP)
            lastkt = min(2 * p2 + 1, cnt - 1)
            return bAV + hi2 * (cnt + 1) + lastkt + 1

        with (
            nc.psum_tensor([128, 2 * SPW], F32) as spp0_,
            nc.psum_tensor([128, 2 * SPW], F32) as spp1_,
            nc.psum_tensor([1, 512], F32) as lps,
            nc.psum_tensor([128, 512], F32) as ops0,
            nc.psum_tensor([128, 512], F32) as ops1,
            nc.psum_tensor([128, 512], F32) as bcp,
            nc.Block() as blk,
        ):
            spp = [spp0_, spp1_]
            opsl = [ops0, ops1]

            @blk.tensor
            def _(te):
                def emit_av(hi, p):
                    gp = hi * PP + p
                    kvh = hi // 4
                    kts = list(range(2 * p, min(2 * p + 2, cnt)))
                    for kt in kts:
                        if kt == 0 and hi >= 1:
                            te.wait_ge(sm["vE"].h, bVE + 3 * hi - 2)
                        if kt >= FF:
                            te.wait_ge(sm["vA"].h, bVA + nmask_before[(hi, kt)] + 1)
                        else:
                            te.wait_ge(sm["aE"].h, bAE + gp + 1)
                        ph = P_t[gp % 2][:, 512 * (kt % 2):512 * (kt % 2) + 512]
                        te.matmul(opsl[hi % 2][:], V_t[:, kt, 128 * kvh:128 * (kvh + 1)],
                                  ph, start=(kt == 0), stop=(kt == cnt - 1))
                    for kt in kts:
                        ph = P_t[gp % 2][:, 512 * (kt % 2):512 * (kt % 2) + 512]
                        te.matmul(lps[:], ones_b[:], ph,
                                  start=(kt == 0), stop=(kt == cnt - 1)
                                  ).then_inc(sm["tAV"].h, 1)

                def emit_bcast(j):
                    te.wait_ge(sm["aS"].h, bAS_A + j + 1)
                    te.matmul(bcp[:], ones_k1[:], lrec[0:1, :], start=True, stop=True
                              ).then_inc(sm["tAV"].h, 1)

                for hi in range(NH):
                    kvh = hi // 4
                    for p in range(PP):
                        gp = hi * PP + p
                        if gp >= 2:
                            te.wait_ge(sm["aE"].h, bAE + gp - 1)
                        for kt in range(2 * p, min(2 * p + 2, cnt)):
                            te.matmul(spp[gp % 2][:, 512 * (kt % 2):512 * (kt % 2) + 512],
                                      K_t[:, kvh, 128 * kt:128 * (kt + 1)],
                                      Q_t[:, hi, :], start=True, stop=True
                                      ).then_inc(sm["tS"].h, 1)
                        if p == 0 and hi >= 1:
                            emit_bcast(hi - 1)
                        if p >= 1:
                            emit_av(hi, p - 1)
                    emit_av(hi, PP - 1)
                emit_bcast(NH - 1)

            @blk.scalar
            def _(sc):
                sc = Chain(sc, "aCH")
                for hi in range(NH):
                    for p in range(PP):
                        gp = hi * PP + p
                        k1 = min(2 * p + 1, cnt - 1)
                        sc.wait_ge(sm["tS"].h, bS + hi * cnt + k1 + 1)
                        if gp >= 2:
                            sc.wait_ge(sm["tAV"].h, tav_after_pair(gp - 2))
                        width = 1024 if 2 * p + 1 < cnt else 512
                        sc.activation(P_t[gp % 2][:, :width], spp[gp % 2][:, :width],
                                      AF.Exp, bias=cb4[:, 0:1], scale=SCALE,
                                      _lag=None)
                        sc.mark("aE")
                        if p == 0 and hi >= 1:
                            sc.wait_ge(sm["vE"].h, bVE + 3 * (hi - 1) + 1)
                            sc.activation(lln[0:1, :], lrow[0:1, :], AF.Ln,
                                          _lag=None)
                            sc.activation(lrec[0:1, :], lln[0:1, :], AF.Exp,
                                          scale=-1.0, _lag=1)
                            sc.mark("aS")
                sc.wait_ge(sm["vE"].h, bVE + 3 * (NH - 1) + 1)
                sc.activation(lln[0:1, :], lrow[0:1, :], AF.Ln, _lag=None)
                sc.activation(lrec[0:1, :], lln[0:1, :], AF.Exp, scale=-1.0, _lag=1)
                sc.mark("aS")
                sc.finish()

            @blk.vector
            def _(v):
                v = Chain(v, "vCH")
                for hi in range(NH):
                    for kt in range(cnt):
                        if kt >= FF:
                            gp = hi * PP + kt // 2
                            v.wait_ge(sm["aE"].h, bAE + gp + 1)
                            mi = kt - FF
                            ph = P_t[gp % 2][:, 512 * (kt % 2):512 * (kt % 2) + 512]
                            v.tensor_mul(ph, ph, masks_t[:, mi, :], _lag=None)
                            v.mark("vA")
                    v.wait_ge(sm["tAV"].h, bAV + hi * (cnt + 1) + cnt)
                    if hi >= 1:
                        v.wait_ge(sm["aS"].h, bAS_A + hi)
                    v.tensor_copy(lrow[0:1, :], lps[0:1, :], _lag=None)
                    v.mark("vE")
                    v.wait_ge(sm["tAV"].h, bAV + (hi + 1) * (cnt + 1))
                    v.tensor_copy(bc0[:], bcp[:], _lag=2)
                    v.mark("vE")
                    v.tensor_mul(On_t[:, hi, :], opsl[hi % 2][:], bc0[:], _lag=1)
                    v.mark("vE")
                v.finish()
        sm["tS"].v += NH * cnt
        sm["tAV"].v += NH * (cnt + 1)
        nc.all_engine_barrier()

        bWq = [sm["dWq0"].v, sm["dWq1"].v, sm["dWq2"].v]
        dWq = [sm["dWq0"].h, sm["dWq1"].h, sm["dWq2"].h]
        bWo, bVO, bAS = sm["tWo"].v, sm["vO"].v, sm["aS"].v
        bXO2 = sm["dXF"].v
        with (
            nc.psum_tensor([128, 512], F32) as aps0,
            nc.psum_tensor([128, 512], F32) as aps1,
            nc.psum_tensor([1, 512], F32) as sum_ps,
            nc.psum_tensor([1, 512], F32) as ssq_ps,
            nc.psum_tensor([128, 512], F32) as mub_ps,
            nc.psum_tensor([128, 512], F32) as rsb_ps,
            nc.Block() as blk,
        ):
            aps = [aps0, aps1]

            @blk.gpsimd
            def _(gp):
                for dc in range(16):
                    if dc >= 3:
                        gp.wait_ge(sm["tWo"].h,
                                   bWo + (1 if dc == 3 else 2 * (dc - 3)))
                    gp.dma_start(out=wblk[dc % 3][:], in_=wo_in[dc]
                                 ).then_inc(dWq[dc % 3], 16)
                if s < NSLOT - 1:
                    gp.dma_start(out=xob16[:], in_=xo_in[s + 1]
                                 ).then_inc(sm["dXO"].h, 16)

            @blk.tensor
            def _(te):
                def emit_sums(dc):
                    te.wait_ge(sm["vO"].h, bVO + dc + 1)
                    te.matmul(sum_ps[:], ones_f[:], xof32[:, dc, :],
                              start=(dc == 0), stop=(dc == 15))
                    te.wait_ge(sm["aS"].h, bAS + dc + 1)
                    te.matmul(ssq_ps[:], ones_f[:], sqs[dc % 2][:],
                              start=(dc == 0), stop=(dc == 15)
                              ).then_inc(sm["tWo"].h, 1)

                for dc in range(16):
                    te.wait_ge(dWq[dc % 3], bWq[dc % 3] + 16 * (dc // 3 + 1))
                    if dc >= 2:
                        te.wait_ge(sm["vO"].h, bVO + (dc - 1))
                    for qh in range(NH):
                        ins = te.matmul(aps[dc % 2][:], wblk[dc % 3][:, qh, :],
                                        On_t[:, qh, :], start=(qh == 0), stop=(qh == 15))
                    ins.then_inc(sm["tWo"].h, 1)
                    if dc >= 1:
                        emit_sums(dc - 1)
                emit_sums(15)
                te.wait_ge(sm["vO"].h, bVO + 16 + 2)
                te.matmul(mub_ps[:], ones_k1[:], r0[0:1, :], start=True, stop=True)
                te.wait_ge(sm["aS"].h, bAS + 18)
                te.matmul(rsb_ps[:], ones_k1[:], r1[0:1, :], start=True, stop=True
                          ).then_inc(sm["tWo"].h, 1)

            @blk.scalar
            def _(sc):
                sc = Chain(sc, "aCH")
                sc.dma_start(out=xof32[:], in_=xo_in[s]).then_inc(sm["dXF"].h, 16)
                for dc in range(16):
                    sc.wait_ge(sm["vO"].h, bVO + dc + 1)
                    if dc >= 2:
                        sc.wait_ge(sm["tWo"].h, bWo + 2 * dc - 1)
                    sc.activation(sqs[dc % 2][:], xof32[:, dc, :], AF.Square,
                                  _lag=None)
                    sc.mark("aS")
                sc.wait_ge(sm["vO"].h, bVO + 16 + 2)
                sc.activation(r2[0:1, :], r1[0:1, :], AF.Ln, bias=ceps1[0:1, 0:1],
                              _lag=None)
                sc.mark("aS")
                sc.activation(r1[0:1, :], r2[0:1, :], AF.Exp, scale=-0.5, _lag=1)
                sc.mark("aS")
                sc.finish()

            @blk.vector
            def _(v):
                v = Chain(v, "vCH")
                v.wait_ge(sm["dXF"].h, bXO2 + 16)
                for dc in range(16):
                    v.wait_ge(sm["tWo"].h, bWo + (1 if dc == 0 else 2 * dc))
                    v.tensor_add(xof32[:, dc, :], aps[dc % 2][:], xof32[:, dc, :],
                                 _lag=None)
                    v.mark("vO")
                v.wait_ge(sm["tWo"].h, bWo + 32)
                v.tensor_scalar_mul(r0[0:1, :], sum_ps[0:1, :], INV_D, _lag=None)
                v.tensor_scalar_mul(r1[0:1, :], ssq_ps[0:1, :], INV_D, _lag=None
                                    )
                v.mark("vO")
                v.tensor_mul(r2[0:1, :], r0[0:1, :], r0[0:1, :], _lag=2)
                v.tensor_sub(r1[0:1, :], r1[0:1, :], r2[0:1, :], _lag=1
                             )
                v.mark("vO")
                v.wait_ge(sm["tWo"].h, bWo + 33)
                v.tensor_copy(bc0[:], mub_ps[:], _lag=None)
                v.tensor_copy(bc1[:], rsb_ps[:], _lag=None)
                for dc in range(0, 16, 2):
                    v.tensor_sub(xof32[:, dc, :], xof32[:, dc, :], bc0[:], _lag=2)
                    v.tensor_sub(xof32[:, dc + 1, :], xof32[:, dc + 1, :], bc0[:],
                                 _lag=2)
                    v.tensor_mul(xof32[:, dc, :], xof32[:, dc, :], bc1[:], _lag=2)
                    v.tensor_mul(xof32[:, dc + 1, :], xof32[:, dc + 1, :], bc1[:],
                                 _lag=2)
                    v.tensor_copy(x2b[:, dc, :], xof32[:, dc, :], _lag=2)
                    v.mark("vO")
                    v.tensor_copy(x2b[:, dc + 1, :], xof32[:, dc + 1, :], _lag=2)
                    v.mark("vO")
                v.finish()
        sm["dWq0"].v += 16 * 6
        sm["dWq1"].v += 16 * 5
        sm["dWq2"].v += 16 * 5
        sm["dXF"].v += 16
        if s < NSLOT - 1:
            sm["dXO"].v += 16
        sm["tWo"].v += 33

        bRL = [sm["dRL0"].v, sm["dRL1"].v, sm["dRLg2"].v]
        dRL = [sm["dRL0"].h, sm["dRL1"].h, sm["dRLg2"].h]
        bGS = [sm["dGS0"].v, sm["dGS1"].v]
        dGS = [sm["dGS0"].h, sm["dGS1"].h]
        bTF, bAS, bVF = sm["tF"].v, sm["aS"].v, sm["vF"].v
        bVO_pre = sm["vO"].v - 34
        with (
            nc.psum_tensor([128, 512], F32) as gps0,
            nc.psum_tensor([128, 512], F32) as gps1,
            nc.psum_tensor([128, 512], F32) as ups0,
            nc.psum_tensor([128, 512], F32) as ups1,
            nc.Block() as blk,
        ):
            gps, ups = [gps0, gps1], [ups0, ups1]

            @blk.sync
            def _(sy):
                for spo in range(22):
                    if spo in (1, 2):
                        sy.wait_ge(sm["vO"].h, bVO_pre + 34)
                    elif spo >= 3:
                        sy.wait_ge(sm["tF"].h, bTF + 2 * (spo - 2))
                    sy.dma_start(out=gslab[spo % 3][:], in_=w3g[spo, :, :, :]
                                 ).then_inc(dRL[spo % 3], 16)

            @blk.tensor
            def _(te):
                for spo in range(22):
                    te.wait_ge(dRL[spo % 3], bRL[spo % 3] + 16 * (spo // 3 + 1))
                    for jj in range(2):
                        pj = 2 * spo + jj
                        if pj >= 2:
                            te.wait_ge(sm["vF"].h, bVF + pj - 1)
                        for c in range(16):
                            if spo == 0 and jj == 0:
                                te.wait_ge(sm["vO"].h, bVO_pre + 19 + c)
                            te.matmul(gps[jj][:], gslab[spo % 3][:, c, 256 * jj:256 * jj + 128],
                                      x2b[:, c, :], start=(c == 0), stop=(c == 15))
                        for c in range(16):
                            ins = te.matmul(ups[jj][:], gslab[spo % 3][:, c, 256 * jj + 128:256 * jj + 256],
                                            x2b[:, c, :], start=(c == 0), stop=(c == 15))
                        ins.then_inc(sm["tF"].h, 1)

            @blk.scalar
            def _(sc):
                sc = Chain(sc, "aCH")

                def emit_silus(sp2):
                    for jj in range(2):
                        pj = 2 * sp2 + jj
                        sc.wait_ge(sm["tF"].h, bTF + pj + 1)
                        if pj >= 2:
                            sc.wait_ge(sm["vF"].h, bVF + pj - 1)
                        sc.activation(silu_sb[pj % 2][:], gps[jj][:], AF.Silu,
                                      _lag=None)
                        sc.mark("aS")

                for sp2 in range(22):
                    emit_silus(sp2)
                sc.finish()

            @blk.vector
            def _(v):
                v = Chain(v, "vCH")
                for spo in range(22):
                    for jj in range(2):
                        pj = 2 * spo + jj
                        v.wait_ge(sm["aS"].h, bAS + pj + 1)
                        v.tensor_mul(h_t[:, pj, :], silu_sb[pj % 2][:], ups[jj][:],
                                     _lag=None)
                        v.mark("vF")
                v.finish()
        sm["dRL0"].v += 16 * 8
        sm["dRL1"].v += 16 * 7
        sm["dRLg2"].v += 16 * 7
        sm["tF"].v += 44
        nc.all_engine_barrier()

        bRL2 = [sm["dRL20"].v, sm["dRL21"].v, sm["dRL22"].v]
        dRL2 = [sm["dRL20"].h, sm["dRL21"].h, sm["dRL22"].h]
        bTF, bVY = sm["tF"].v, sm["vY"].v
        with (
            nc.psum_tensor([128, 512], F32) as yp0,
            nc.psum_tensor([128, 512], F32) as yp1,
            nc.psum_tensor([128, 512], F32) as yp2,
            nc.psum_tensor([128, 512], F32) as yp3,
            nc.psum_tensor([128, 512], F32) as yp4,
            nc.psum_tensor([128, 512], F32) as yp5,
            nc.psum_tensor([128, 512], F32) as yp6,
            nc.psum_tensor([128, 512], F32) as yp7,
            nc.Block() as blk,
        ):
            yps = [yp0, yp1, yp2, yp3, yp4, yp5, yp6, yp7]

            @blk.sync
            def _(sy):
                for grp in range(2):
                    for jc in range(44):
                        gj = grp * 44 + jc
                        if gj >= 3:
                            sy.wait_ge(sm["tF"].h, bTF + gj - 2)
                        sy.dma_start(out=dslab[gj % 3][:],
                                     in_=w3d[grp, jc]
                                     ).then_inc(dRL2[gj % 3], 16)

            @blk.tensor
            def _(te):
                for grp in range(2):
                    if grp == 1:
                        te.wait_ge(sm["vY"].h, bVY + 8)
                    for jc in range(44):
                        gj = grp * 44 + jc
                        te.wait_ge(dRL2[gj % 3], bRL2[gj % 3] + 16 * (gj // 3 + 1))
                        for k8 in range(8):
                            dc = grp * 8 + k8
                            ins = te.matmul(yps[k8][:], dslab[gj % 3][:, 128 * k8:128 * (k8 + 1)],
                                            h_t[:, jc, :], start=(jc == 0), stop=(jc == 43))
                        ins.then_inc(sm["tF"].h, 1)

            @blk.vector
            def _(v):
                v = Chain(v, "vCH")
                for grp in range(2):
                    v.wait_ge(sm["tF"].h, bTF + 44 * (grp + 1))
                    for k8 in range(8):
                        v.tensor_copy(yf[:, grp * 8 + k8, :], yps[k8][:], _lag=None)
                        v.mark("vY")
                v.finish()
        sm["dRL20"].v += 16 * 30
        sm["dRL21"].v += 16 * 29
        sm["dRL22"].v += 16 * 29
        sm["tF"].v += 88

        bTF2, bVY2, bAS2 = sm["tF"].v, sm["vY"].v, sm["aS"].v
        with (
            nc.psum_tensor([1, 512], F32) as sq1_ps,
            nc.psum_tensor([1, 512], F32) as sum2_ps,
            nc.psum_tensor([1, 512], F32) as sq2_ps,
            nc.psum_tensor([128, 512], F32) as b0_ps,
            nc.psum_tensor([128, 512], F32) as b1_ps,
            nc.Block() as blk,
        ):
            @blk.scalar
            def _(sc):
                sc = Chain(sc, "aCH")
                for dc in range(16):
                    sc.wait_ge(sm["vY"].h, bVY2 - 16 + dc + 1)
                    if dc >= 2:
                        sc.wait_ge(sm["tF"].h, bTF2 + dc - 1)
                    sc.activation(sqs[dc % 2][:], yf[:, dc, :], AF.Square,
                                  _lag=None)
                    sc.mark("aS")
                sc.wait_ge(sm["vY"].h, bVY2 + 1)
                sc.activation(r2[0:1, :], r1[0:1, :], AF.Ln, bias=ceps2[0:1, 0:1],
                              _lag=None)
                sc.mark("aS")
                sc.activation(r1[0:1, :], r2[0:1, :], AF.Exp, scale=-0.5, _lag=1)
                sc.mark("aS")
                for dc in range(16):
                    sc.wait_ge(sm["vY"].h, bVY2 + 1 + dc + 1)
                    if dc >= 2:
                        sc.wait_ge(sm["tF"].h, bTF2 + 16 + dc)
                    sc.activation(sqs[dc % 2][:], yf[:, dc, :], AF.Square,
                                  _lag=None)
                    sc.mark("aS")
                sc.wait_ge(sm["vY"].h, bVY2 + 1 + 16 + 2)
                sc.activation(r2[0:1, :], r1[0:1, :], AF.Ln, bias=ceps1[0:1, 0:1],
                              _lag=None)
                sc.mark("aS")
                sc.activation(r1[0:1, :], r2[0:1, :], AF.Exp, scale=-0.5, _lag=1)
                sc.mark("aS")
                sc.finish()

            @blk.tensor
            def _(te):
                te.wait_ge(sm["vY"].h, bVY2)
                for dc in range(16):
                    te.wait_ge(sm["aS"].h, bAS2 + dc + 1)
                    te.matmul(sq1_ps[:], ones_f[:], sqs[dc % 2][:],
                              start=(dc == 0), stop=(dc == 15)).then_inc(sm["tF"].h, 1)
                te.wait_ge(sm["aS"].h, bAS2 + 18)
                te.matmul(b0_ps[:], ones_k1[:], r1[0:1, :], start=True, stop=True
                          ).then_inc(sm["tF"].h, 1)
                for dc in range(16):
                    te.wait_ge(sm["vY"].h, bVY2 + 1 + dc + 1)
                    te.matmul(sum2_ps[:], ones_f[:], yf[:, dc, :],
                              start=(dc == 0), stop=(dc == 15))
                    te.wait_ge(sm["aS"].h, bAS2 + 18 + dc + 1)
                    te.matmul(sq2_ps[:], ones_f[:], sqs[dc % 2][:],
                              start=(dc == 0), stop=(dc == 15)).then_inc(sm["tF"].h, 1)
                te.wait_ge(sm["vY"].h, bVY2 + 1 + 16 + 2)
                te.matmul(b0_ps[:], ones_k1[:], r0[0:1, :], start=True, stop=True)
                te.wait_ge(sm["aS"].h, bAS2 + 36)
                te.matmul(b1_ps[:], ones_k1[:], r1[0:1, :], start=True, stop=True
                          ).then_inc(sm["tF"].h, 1)

            @blk.vector
            def _(v):
                v = Chain(v, "vCH")
                v.wait_ge(sm["tF"].h, bTF2 + 16)
                v.tensor_scalar_mul(r1[0:1, :], sq1_ps[0:1, :], INV_D, _lag=None)
                v.mark("vY")
                v.wait_ge(sm["tF"].h, bTF2 + 17)
                v.tensor_copy(bc0[:], b0_ps[:], _lag=None)
                for dc in range(0, 16, 2):
                    v.tensor_mul(xof32[:, dc, :], yf[:, dc, :], bc0[:],
                                 _lag=1 if dc == 0 else 2)
                    v.tensor_mul(xof32[:, dc + 1, :], yf[:, dc + 1, :], bc0[:],
                                 _lag=2)
                    v.tensor_add(yf[:, dc, :], xof32[:, dc, :], x2b[:, dc, :],
                                 _lag=2)
                    v.mark("vY")
                    v.tensor_add(yf[:, dc + 1, :], xof32[:, dc + 1, :],
                                 x2b[:, dc + 1, :], _lag=2)
                    v.mark("vY")
                v.wait_ge(sm["tF"].h, bTF2 + 33)
                v.tensor_scalar_mul(r0[0:1, :], sum2_ps[0:1, :], INV_D, _lag=None)
                v.tensor_scalar_mul(r1[0:1, :], sq2_ps[0:1, :], INV_D, _lag=None)
                v.mark("vY")
                v.tensor_mul(r2[0:1, :], r0[0:1, :], r0[0:1, :], _lag=2)
                v.tensor_sub(r1[0:1, :], r1[0:1, :], r2[0:1, :], _lag=1)
                v.mark("vY")
                v.wait_ge(sm["tF"].h, bTF2 + 34)
                v.tensor_copy(bc0[:], b0_ps[:], _lag=None)
                v.tensor_copy(bc1[:], b1_ps[:], _lag=None)
                for dc in range(0, 16, 2):
                    v.tensor_sub(xof32[:, dc, :], yf[:, dc, :], bc0[:], _lag=2)
                    v.tensor_sub(xof32[:, dc + 1, :], yf[:, dc + 1, :], bc0[:],
                                 _lag=2)
                    v.tensor_mul(yf[:, dc, :], xof32[:, dc, :], bc1[:], _lag=2)
                    v.mark("vY")
                    v.tensor_mul(yf[:, dc + 1, :], xof32[:, dc + 1, :], bc1[:],
                                 _lag=2)
                    v.mark("vY")
                v.finish()

            @blk.sync
            def _(sy):
                sy.wait_ge(sm["vY"].h, bVY2 + 35)
                sy.dma_start(out=out_t[:, :, SPW * s:SPW * (s + 1)], in_=yf[:]
                             ).then_inc(sm["dOUT"].h, 16)
        sm["tF"].v += 34
        sm["dOUT"].v += 16
        nc.all_engine_barrier()

    with nc.Block() as blk:
        @blk.sync
        def _(sy):
            sy.wait_ge(sm["dOUT"].h, sm["dOUT"].v)
    return nc


def _rope_tables():
    inv_freq = 1.0 / (10000.0 ** (np.arange(0, DH, 2, dtype=np.float32) / DH))
    freqs = np.arange(S, dtype=np.float32)[:, None] * inv_freq[None, :]
    cos = np.repeat(np.cos(freqs), 2, axis=-1).astype(np.float32).T.copy()
    sin = np.repeat(np.sin(freqs), 2, axis=-1).astype(np.float32).T.copy()
    sin_eff = sin.copy()
    sin_eff[:64] *= -1.0
    return cos, sin_eff


def _mask_tiles(q0s):
    out = np.zeros((NSLOT, 128, 8, SPW), dtype=BF)
    kk = np.arange(128)[:, None]
    qq = np.arange(SPW)[None, :]
    for s_ in range(NSLOT):
        q0 = q0s[s_]
        FF = SLOT_F[s_]
        for m in range(8):
            kt = FF + m
            if kt >= SLOT_COUNTS[s_]:
                continue
            k0 = kt * 128
            off = (k0 - q0) // 128
            if off <= 0:
                out[s_, :, m, :] = 1.0
            else:
                out[s_, :, m, :] = ((k0 + kk) <= (q0 + qq + WIN)).astype(np.float32)
    return out


def kernel(L0_attn_x, L0_attn_Wq, L0_attn_Wk, L0_attn_Wv, L0_attn_Wo,
           L0_ffn_gate_up_packed, L0_ffn_gate_up_scales,
           L0_ffn_down_packed, L0_ffn_down_scales):
    x = np.ascontiguousarray(np.asarray(L0_attn_x, dtype=np.float32))
    wq = np.asarray(L0_attn_Wq, dtype=np.float32)
    wk = np.asarray(L0_attn_Wk, dtype=np.float32)
    wv = np.asarray(L0_attn_Wv, dtype=np.float32)
    wo = np.asarray(L0_attn_Wo, dtype=np.float32)
    gup = np.asarray(L0_ffn_gate_up_packed, dtype=np.int32)
    gus = np.asarray(L0_ffn_gate_up_scales)
    dnp = np.asarray(L0_ffn_down_packed, dtype=np.int32)
    dns = np.asarray(L0_ffn_down_scales)

    cosk, sink = _rope_tables()

    wqT = wq.T[pi_d]
    wq_pack = np.ascontiguousarray(
        wqT.reshape(16, 128, NH, 128).transpose(2, 1, 0, 3))
    wkT = wk.T[pi_d]
    wk_pack = np.ascontiguousarray(wkT.reshape(16, 128, 512).transpose(1, 0, 2))
    wvT = wv.T[pi_d]
    wv_pack = np.ascontiguousarray(wvT.reshape(16, 128, 512).transpose(1, 0, 2))
    woT = wo.T[:, pi_d]
    wo_pack = np.ascontiguousarray(
        woT.reshape(NH, 128, 16, 128).transpose(2, 1, 0, 3))
    gate_p, up_p = gup[:II][pi_j], gup[II:][pi_j]
    gusT = np.asarray(gus, np.float16).T
    gate_s, up_s = gusT[:II][pi_j], gusT[II:][pi_j]
    gup_pack = np.empty((88, 128, 256), np.int32)
    gus_pack = np.empty((88, 128, 16), np.float16)
    for j in range(44):
        gup_pack[2 * j] = gate_p[128 * j:128 * (j + 1)]
        gup_pack[2 * j + 1] = up_p[128 * j:128 * (j + 1)]
        gus_pack[2 * j] = gate_s[128 * j:128 * (j + 1)]
        gus_pack[2 * j + 1] = up_s[128 * j:128 * (j + 1)]
    dnp_pack = np.ascontiguousarray(dnp[pi_d].reshape(16, 128, 704))
    dns_pack = np.ascontiguousarray(
        np.asarray(dns, np.float16).T[pi_d].reshape(16, 128, 44))
    pswap_np = np.zeros((128, 128), dtype=BF)
    for a in range(128):
        pswap_np[(a + 64) % 128, a] = 1.0
    ident_np = np.eye(128, dtype=np.float32).astype(BF)
    cosk_np = np.ascontiguousarray(cosk)
    sink_np = np.ascontiguousarray(sink)


    nc = build_program()

    in_maps = []
    span_sets = [SPANS_A, SPANS_B]
    out_meta = []
    for core in range(8):
        b, half = divmod(core, 2)
        spans = span_sets[half]
        own_cols = np.concatenate([np.arange(sp * SPW, (sp + 1) * SPW) for sp in spans])
        q0s = [sp * SPW for sp in spans]
        xT = x[b].T[pi_d]
        x_pack = np.ascontiguousarray(
            xT.reshape(16, 128, 8, SPW).transpose(2, 1, 0, 3))
        xo = xT[:, own_cols]
        xo_pack = np.ascontiguousarray(
            xo.reshape(16, 128, NSLOT, SPW).transpose(2, 1, 0, 3))
        cosq_np = np.ascontiguousarray(
            cosk[:, own_cols].reshape(128, NSLOT, SPW).transpose(1, 0, 2))
        sinq_np = np.ascontiguousarray(
            sink[:, own_cols].reshape(128, NSLOT, SPW).transpose(1, 0, 2))
        masks_np = _mask_tiles(q0s)
        in_maps.append({
            "x_in": x_pack, "xo_in": xo_pack,
            "wq_in": wq_pack, "wo_in": wo_pack, "wk_in": wk_pack, "wv_in": wv_pack,
            "gup_in": gup_pack, "gus_in": gus_pack,
            "dnp_in": dnp_pack, "dns_in": dns_pack,
            "cosk_in": cosk_np, "sink_in": sink_np,
            "cosq_in": cosq_np, "sinq_in": sinq_np,
            "pswap_in": pswap_np, "masks_in": masks_np, "ident_in": ident_np,
        })
        out_meta.append((b, own_cols))

    res = run_bass_kernel_spmd(nc, in_maps, core_ids=list(range(8)))

    out = np.empty((4, S, D), np.float32)
    for core in range(8):
        b, own_cols = out_meta[core]
        ot = np.asarray(res.results[core]["out_t"])
        operm = ot.transpose(1, 0, 2).reshape(D, 4 * SPW)
        ounp = np.empty_like(operm)
        ounp[pi_d] = operm
        out[b, own_cols, :] = ounp.T
    return out
